# version 17
# baseline (speedup 1.0000x reference)
import contextlib
import numpy as np
import ml_dtypes
import concourse.bass as bass
import concourse.mybir as mybir
from concourse.bass_utils import run_bass_kernel_spmd

F32 = mybir.dt.float32
BF16 = mybir.dt.bfloat16
AF = mybir.ActivationFunctionType
ALU = mybir.AluOpType
AX = mybir.AxisListType

D = 4096
KV = 1024
NH = 8
NKC = 32
TOWN = 1024
TFULL = 2048
NSC = 4
SCT = 512
NQT = 8
R = 64
INVSQ = float(1.0 / np.sqrt(128.0))
EPS = 1e-5

SBUF_BASE = 16512
KiB = 1024

ENGINES = ("sync", "scalar", "vector", "tensor", "gpsimd")


class Sem:
    __slots__ = ("h", "n")

    def __init__(self, h):
        self.h = h
        self.n = 0


class P:

    def __init__(self, nc):
        self.nc = nc
        self.q = {e: [] for e in ENGINES}

    def emit(self, eng, fn):
        self.q[eng].append(fn)

    def dma(self, eng, sem, out, in_):
        sem.n += 16
        self.emit(eng, lambda e, o=out, i=in_, h=sem.h: e.dma_start(out=o, in_=i).then_inc(h, 16))
        return sem.n

    def wait(self, eng, sem, val=None):
        v = sem.n if val is None else val
        if v is None or v <= 0:
            return
        self.emit(eng, lambda e, h=sem.h, v=v: e.wait_ge(h, v))

    def op(self, eng, name, *args, sem=None, **kw):
        if sem is not None:
            sem.n += 1
        s = sem

        def f(e, name=name, args=args, kw=kw, s=s):
            ins = getattr(e, name)(*args, **kw)
            if s is not None:
                ins.then_inc(s.h, 1)
        self.emit(eng, f)
        return None if s is None else s.n


def build_nc(taps=(), upto="C"):
    nc = bass.Bass("TRN2", target_bir_lowering=False)
    phase_order = ["P", "A1", "A2", "A3", "B", "C"]
    upto_i = phase_order.index(upto)

    def want(ph):
        return phase_order.index(ph) <= upto_i

    x_own = nc.dram_tensor("x_own", [TOWN, D], F32, kind="ExternalInput").ap()
    x_oth = nc.dram_tensor("x_oth", [TOWN, D], F32, kind="ExternalInput").ap()
    wq = nc.dram_tensor("wq", [D, D], F32, kind="ExternalInput").ap()
    wk = nc.dram_tensor("wk", [D, KV], F32, kind="ExternalInput").ap()
    wv = nc.dram_tensor("wv", [D, KV], F32, kind="ExternalInput").ap()
    wo = nc.dram_tensor("wo", [KV, D], F32, kind="ExternalInput").ap()
    ker = nc.dram_tensor("ker", [D, D], F32, kind="ExternalInput").ap()
    bias_b = nc.dram_tensor("bias_b", [128, D], F32, kind="ExternalInput").ap()
    gate_bi = nc.dram_tensor("gate_b", [128, 16], F32, kind="ExternalInput").ap()
    lat = nc.dram_tensor("lat", [16, 128, 1024], F32, kind="ExternalInput").ap()
    lbt = nc.dram_tensor("lbt", [16, 128, 1024], F32, kind="ExternalInput").ap()
    last = nc.dram_tensor("last", [D, 32], F32, kind="ExternalInput").ap()
    lbst = nc.dram_tensor("lbst", [32, D], F32, kind="ExternalInput").ap()

    out = nc.dram_tensor("out", [TOWN, D], F32, kind="ExternalOutput").ap()

    s_wqe = nc.dram_tensor("s_wqe", [NKC, 128, KV], BF16, kind="Internal").ap()
    s_xn = nc.dram_tensor("s_xn", [NSC, NKC, 128, SCT], BF16, kind="Internal").ap()
    s_xn2 = nc.dram_tensor("s_xn2", [2, NKC, 128, SCT], BF16, kind="Internal").ap()
    s_xnew = nc.dram_tensor("s_xnew", [TOWN, D], F32, kind="Internal").ap()
    s_tmpa = nc.dram_tensor("s_tmpa", [D, 32], BF16, kind="Internal").ap()
    s_tmpb = nc.dram_tensor("s_tmpb", [32, D], BF16, kind="Internal").ap()

    ident = nc.inline_tensor(np.eye(128, dtype=ml_dtypes.bfloat16), name="ident").ap()

    tap = {}
    for name, shape, dt in [
        ("K", [128, NH * TFULL], BF16), ("V", [128, 16 * KV], BF16),
        ("Q", [128, NH * TOWN], BF16), ("XN", [NSC, NKC, 128, SCT], BF16),
        ("XNEW", [TOWN, D], F32), ("XN2", [2, NKC, 128, SCT], BF16),
        ("WQE", [NKC, 128, KV], BF16),
        ("P0", [128, TFULL], BF16), ("O0", [128, NH * 128], BF16),
        ("XR0", [128, D], F32), ("PT0", [128, 16 * 128], BF16),
    ]:
        if name in taps:
            tap[name] = nc.dram_tensor("tap_" + name, shape, dt, kind="ExternalOutput").ap()

    def sb(name, shape, dt, off_kib):
        return nc.alloc_sbuf_tensor_at(name, shape, dt, offset=SBUF_BASE + int(off_kib * KiB)).ap()

    xnbuf = sb("xnbuf", [128, NKC * SCT], BF16, 0)
    V_tm = sb("V_tm", [128, 16 * KV], BF16, 32)
    xbuf = [sb("xbuf0", [128, D], F32, 32),
            sb("xbuf1", [128, D], F32, 48)]
    xn_tm = sb("xn_tm", [128, D], BF16, 64)
    xresA = sb("xresA", [128, 2048], F32, 64)
    Q_fm = sb("Q_fm", [128, NH * TOWN], BF16, 72)
    xnsA = sb("xnsA", [128, 16 * SCT], BF16, 72)
    xnsB = sb("xnsB", [128, 8 * SCT], BF16, 88)
    P_tm = [sb("P_tm0", [128, TFULL], BF16, 88),
            sb("P_tm1", [128, TFULL], BF16, 92)]
    xn2dA = sb("xn2dA", [128, 2048], BF16, 64)
    xn2dB = sb("xn2dB", [128, 2048], BF16, 192)
    W64 = sb("W64", [128, NKC * KV], BF16, 96)
    K_fm = sb("K_fm", [128, NH * TFULL], BF16, 160)
    xnsC = sb("xnsC", [128, 8 * SCT], BF16, 192)
    xresB = sb("xresB", [128, 2048], F32, 192)
    wqb = [sb("wqb0", [128, D], BF16, 72), sb("wqb1", [128, D], BF16, 80)]
    ptmpb = sb("ptmpb", [128, KV], BF16, 88)
    wqec = [sb("wqec0", [128, KV], BF16, 90), sb("wqec1", [128, KV], BF16, 92)]
    wqecP = sb("wqecP", [128, 2 * KV], BF16, 90)
    ker_bf = sb("ker_bf", [128, NKC * 2048], BF16, 32)
    xnb = [sb("xnb0", [128, 2048], F32, 160), sb("xnb1", [128, 2048], F32, 168)]
    bias_h = sb("bias_h", [128, 2048], F32, 176)
    A_catT = sb("A_catT", [128, NKC * R], BF16, 184)
    B_catT = sb("B_catT", [R, D], BF16, 188)
    Ax_bf = sb("Ax_bf", [R, TOWN], BF16, 196)
    idn = sb("idn", [128, 128], BF16, 200)
    gate_b = sb("gate_sb", [128, 16], F32, 200.25)
    stats = sb("stats", [128, 64], F32, 200.5)
    ceps = sb("ceps", [128, 1], F32, 200.40625)
    O_fm = sb("O_fm", [128, NH * 128], BF16, 201)
    PT_sb = sb("PT_sb", [128, 16 * 128], BF16, 203)
    lst_b = [sb(f"lstb{i}", [128, KV], BF16, 88 + 2 * i) for i in range(3)]

    psA = nc.alloc_psum_tensor("psA", [128, 4 * 512], F32).ap()
    psT = nc.alloc_psum_tensor("psT", [128, 2048], BF16).ap()
    psO = nc.alloc_psum_tensor("psO", [128, 512], F32).ap()
    psW = nc.alloc_psum_tensor("psW", [128, 512], F32).ap()

    p = P(nc)
    xnbuf3 = xnbuf.rearrange("p (kc t) -> p kc t", t=SCT)
    psT4 = psT.rearrange("p (b i d) -> p b i d", b=2, d=128)

    with contextlib.ExitStack() as stx:
        mk = lambda nm: Sem(stx.enter_context(nc.semaphore(nm)))
        qs = mk("qs")
        qt_ = mk("qt")
        qg = mk("qg")
        sv = mk("sv")
        ss = mk("ss")
        te = mk("te")
        qw = mk("qw")
        qp = [mk("qp0"), mk("qp1")]
        qsw = [mk("qsw0"), mk("qsw1")]
        qxn = [mk("qxn0"), mk("qxn1")]
        qk = [mk(f"qk{i}") for i in range(4)]
        ql = [mk("ql0"), mk("ql1")]
        qts = mk("qts")
        qb = [mk("qb0"), mk("qb1")]
        qx2 = mk("qx2")
        qx = [mk("qx0"), mk("qx1")]

        psT_guard = [None, None]

        def wait_psT(eng, bank):
            g = psT_guard[bank]
            if g is not None:
                p.wait(eng, g[0], g[1])

        p.dma("sync", qs, idn[:, :], ident)
        p.dma("sync", qs, gate_b[:, :], gate_bi)
        eps_set = p.op("vector", "memset", ceps[:, :], EPS, sem=sv)
        p.wait("scalar", sv, eps_set)

        a1_done_v = a1_done_t = a2_done_v = a2_done_t = None
        a3_done_t = a3_done_s = b_done_t = b_done_v = b_done_s = None
        xn_store = {}
        xn2_store = {}

        presum_done = {}
        wqe_load = {}
        wqe_stored = {}

        def emit_presum_load(kc):
            buf = wqb[kc % 2]
            if kc >= 2:
                p.wait("gpsimd", sv, presum_done[kc - 2])
            wqe_load[kc] = p.dma("gpsimd", qp[kc % 2], buf[:, :], wq[kc * 128:(kc + 1) * 128, :])

        def emit_presum_compute(kc):
            buf = wqb[kc % 2]
            p.wait("vector", qp[kc % 2], wqe_load[kc])
            v4 = buf.rearrange("p (h g d) -> p g h d", g=4, d=128)
            pt3 = ptmpb.rearrange("p (h d) -> p h d", d=128)
            p.op("vector", "tensor_add", pt3, v4[:, 0], v4[:, 1])
            p.op("vector", "tensor_add", pt3, pt3, v4[:, 2])
            wc = wqec[kc % 2]
            if kc >= 2:
                g = wqe_stored.get(kc - 1 if kc % 2 == 0 else kc - 2)
                if g is not None:
                    p.wait("vector", qsw[0], g)
            presum_done[kc] = p.op("vector", "tensor_add",
                                   wc.rearrange("p (h d) -> p h d", d=128),
                                   pt3, v4[:, 3], sem=sv)
            if kc % 2 == 1:
                p.wait("sync", sv, presum_done[kc])
                wqe_stored[kc] = p.dma(
                    "sync", qsw[0],
                    s_wqe[kc - 1:kc + 1].rearrange("two p n -> p two n"),
                    wqecP.rearrange("p (two n) -> p two n", two=2))

        def ln_chunk(pieces):
            mean, var, tvar, tsd, rstd, negmr = (stats[:, i:i + 1] for i in range(6))
            mv = stats[:, 0:2]
            k = 0
            bl = None
            for src, dump in pieces:
                w = (D // len(pieces)) // 512
                for t0 in range(w):
                    bl = p.op("vector", "bn_stats", stats[:, 16 + 6 * k:22 + 6 * k],
                              src[:, t0 * 512:(t0 + 1) * 512], sem=sv)
                    k += 1
            p.wait("vector", sv, bl)
            ag = p.op("vector", "bn_aggr", mv, stats[:, 16:16 + 6 * k], sem=sv)
            p.wait("vector", sv, ag)
            v2 = p.op("vector", "tensor_scalar", tvar, var, 1.0, EPS,
                      ALU.mult, ALU.add, sem=sv)
            p.wait("scalar", sv, v2)
            s3 = p.op("scalar", "activation", tsd, tvar, AF.Sqrt, sem=ss)
            p.wait("vector", ss, s3)
            rv = p.op("vector", "reciprocal", rstd, tsd, sem=sv)
            p.wait("vector", sv, rv)
            v3 = p.op("vector", "tensor_scalar", negmr, mean, rstd, -1.0,
                      ALU.mult, ALU.mult, sem=sv)
            p.wait("scalar", sv, v3)
            r = None
            for src, dump in pieces:
                r = p.op("scalar", "activation", dump, src, AF.Identity,
                         scale=rstd, bias=negmr, sem=ss)
            return r

        def transpose_chunk(dump_bf, dst3, c, norm_done):
            tv = ev = None
            for g in range(4):
                bank = g % 2
                p.wait("tensor", ss, norm_done)
                wait_psT("tensor", bank)
                for i in range(8):
                    kc = g * 8 + i
                    tv = p.op("tensor", "transpose",
                              psT[:, bank * 1024 + i * 128: bank * 1024 + (i + 1) * 128],
                              dump_bf[:, kc * 128:(kc + 1) * 128], idn[:, :], sem=te)
                p.wait("scalar", te, tv)
                dst = dst3[:, g * 8:(g + 1) * 8, c * 128:(c + 1) * 128]
                ev = p.op("scalar", "activation", dst, psT4[:, bank], AF.Copy, sem=ss)
                psT_guard[bank] = (ss, ev)
            return tv, ev

        if want("A1"):
            emit_presum_load(0)
            emit_presum_load(1)
            wk3o = W64.rearrange("p (kc n) -> p kc n", n=KV)
            wk3i = wk.rearrange("(kc p) n -> p kc n", p=128)
            for qtr in range(4):
                p.dma("gpsimd", qw, wk3o[:, qtr * 8:(qtr + 1) * 8], wk3i[:, qtr * 8:(qtr + 1) * 8])
            wk_loaded = qw.n

            kev = {}
            kmm_done = {}
            ln_done = {}
            trans_done = {}
            for i in range(16):
                s, c = divmod(i, 4)
                src = x_own if i < 8 else x_oth
                row = (i % 8) * 128
                if i >= 2:
                    p.wait("scalar", ss, ln_done[i - 2])
                xld = p.dma("scalar", qx[i % 2], xbuf[i % 2][:, :], src[row:row + 128, :])
                p.wait("scalar", qx[i % 2], xld)
                p.wait("vector", qx[i % 2], xld)
                if i >= 1:
                    p.wait("scalar", te, trans_done[i - 1])
                ln_done[i] = ln_chunk([(xbuf[i % 2][:, :], xn_tm[:, :])])
                if i > 0:
                    emit_presum_compute(2 * (i - 1))
                    emit_presum_compute(2 * (i - 1) + 1)
                    emit_presum_load(2 * i)
                    emit_presum_load(2 * i + 1)
                if c == 0 and s >= 1:
                    p.wait("scalar", te, kmm_done[s - 1])
                    p.wait("scalar", qt_, xn_store[s - 1])
                trans_done[i], _ = transpose_chunk(xn_tm[:, :], xnbuf3, c, ln_done[i])
                if c == 3:
                    p.wait("sync", ss)
                    xn_store[s] = p.dma("sync", qt_, s_xn[s].rearrange("kc p t -> p kc t"),
                                        xnbuf3)
                    p.wait("tensor", ss)
                    p.wait("tensor", qw, wk_loaded)
                    for j in range(NH):
                        bank = j % 2
                        if j >= 2 and kev.get(j - 2) is not None:
                            p.wait("tensor", ss, kev[j - 2])
                        mm = None
                        for kc in range(NKC):
                            mm = p.op("tensor", "matmul",
                                      psA[:, bank * 512:(bank + 1) * 512],
                                      W64[:, kc * KV + j * 128: kc * KV + (j + 1) * 128],
                                      xnbuf[:, kc * SCT:(kc + 1) * SCT],
                                      start=(kc == 0), stop=(kc == NKC - 1), sem=te)
                        kmm_done[s] = mm
                        p.wait("scalar", te, mm)
                        kev[j] = p.op("scalar", "activation",
                                      K_fm[:, j * TFULL + s * SCT: j * TFULL + (s + 1) * SCT],
                                      psA[:, bank * 512:(bank + 1) * 512], AF.Copy, sem=ss)
            emit_presum_compute(NKC - 2)
            emit_presum_compute(NKC - 1)
            a1_done_v = sv.n
            a1_done_s = ss.n
            a1_done_t = te.n
            p_done_v = presum_done[NKC - 1]
            p_st = wqe_stored[NKC - 1]
            if "WQE" in tap:
                p.wait("sync", qsw[0], p_st)
                for kc in range(NKC):
                    p.dma("sync", qt_, tap["WQE"][kc], s_wqe[kc])
            if "XN" in tap:
                p.wait("sync", qt_)
                for s in range(NSC):
                    p.dma("sync", qt_, tap["XN"][s], s_xn[s])
            if "K" in tap:
                p.wait("sync", ss, a1_done_s)
                p.dma("sync", qt_, tap["K"], K_fm[:, :])

        if want("A2"):
            p.wait("gpsimd", te, a1_done_t)
            wv3o = W64.rearrange("p (kc n) -> p kc n", n=KV)
            wv3i = wv.rearrange("(kc p) n -> p kc n", p=128)
            for qtr in range(4):
                p.dma("gpsimd", qw, wv3o[:, qtr * 8:(qtr + 1) * 8], wv3i[:, qtr * 8:(qtr + 1) * 8])
            wv_loaded = qw.n

            def xn_view(par, kc):
                if par == 0:
                    return xnbuf[:, kc * SCT:(kc + 1) * SCT]
                if kc < 16:
                    return xnsA[:, kc * SCT:(kc + 1) * SCT]
                if kc < 24:
                    return xnsB[:, (kc - 16) * SCT:(kc - 15) * SCT]
                return xnsC[:, (kc - 24) * SCT:(kc - 23) * SCT]

            def xn_load(par, s, guard_te):
                p.wait("sync", qt_, xn_store[s])
                if guard_te is not None:
                    p.wait("sync", te, guard_te)
                if par != 0:
                    p.wait("sync", sv, p_done_v)
                    p.wait("sync", qsw[0], p_st)
                src = s_xn[s].rearrange("kc p t -> p kc t")
                if par == 0:
                    p.dma("sync", qxn[0], xnbuf3, src)
                else:
                    p.dma("sync", qxn[1], xnsA.rearrange("p (kc t) -> p kc t", t=SCT), src[:, 0:16])
                    p.dma("sync", qxn[1], xnsB.rearrange("p (kc t) -> p kc t", t=SCT), src[:, 16:24])
                    p.dma("sync", qxn[1], xnsC.rearrange("p (kc t) -> p kc t", t=SCT), src[:, 24:32])
                return qxn[par].n

            vload = {}
            vev = {}
            vmm_done = {}
            vload[0] = xn_load(0, 0, a1_done_t)
            first_vmm = True
            for s in range(NSC):
                par = s % 2
                if s + 1 < NSC:
                    vload[s + 1] = xn_load((s + 1) % 2, s + 1, vmm_done.get(s - 1))
                p.wait("tensor", qxn[s % 2], vload[s])
                p.wait("tensor", qw, wv_loaded)
                if first_vmm:
                    p.wait("tensor", ss, a1_done_s)
                    first_vmm = False
                for t in range(4):
                    blk = s * 4 + t
                    b0 = 0 if t % 2 == 0 else 2
                    for n0 in (0, 1):
                        g = vev.get((blk - 2, n0))
                        if g is not None:
                            p.wait("tensor", sv, g)
                    mm = None
                    for kc in range(NKC):
                        lhs = xn_view(par, kc)[:, t * 128:(t + 1) * 128]
                        for n0 in (0, 1):
                            bank = b0 + n0
                            mm = p.op("tensor", "matmul",
                                      psA[:, bank * 512:(bank + 1) * 512],
                                      lhs, W64[:, kc * KV + n0 * 512: kc * KV + (n0 + 1) * 512],
                                      start=(kc == 0), stop=(kc == NKC - 1), sem=te)
                    vmm_done[s] = mm
                    p.wait("vector", te, mm)
                    for n0 in (0, 1):
                        bank = b0 + n0
                        vev[(blk, n0)] = p.op(
                            "vector", "tensor_copy",
                            V_tm[:, blk * KV + n0 * 512: blk * KV + (n0 + 1) * 512],
                            psA[:, bank * 512:(bank + 1) * 512], sem=sv)
            a2_done_v = sv.n
            a2_done_t = te.n
            if "V" in tap:
                p.wait("sync", sv, a2_done_v)
                p.dma("sync", qt_, tap["V"], V_tm[:, :])

        accP = psT.bitcast(F32)
        lat2 = [lat, lbt]
        comb_done = {}
        lst_ld = {}
        tmp_store = [None, None]
        comb_k = [0]
        nlst = 2

        def emit_comb_steps(n_steps):
            for _ in range(n_steps):
                k = comb_k[0]
                if k >= 32:
                    return
                comb_k[0] += 1
                ph, n = divmod(k, 16)
                buf = lst_b[k % nlst]
                if k == 0:
                    p.wait("gpsimd", te, a2_done_t)
                    p.wait("gpsimd", sv, p_done_v)
                    p.wait("gpsimd", qsw[0], p_st)
                if k >= nlst:
                    p.wait("gpsimd", sv, comb_done[k - nlst])
                lst_ld[k] = p.dma("gpsimd", ql[k % 2], buf[:, :], lat2[ph][n])
                p.wait("vector", ql[k % 2], lst_ld[k])
                if n == 0:
                    if ph == 0:
                        p.wait("vector", qs, 32)
                        for bank in range(2):
                            wait_psT("vector", bank)
                    comb_done[k] = p.op("vector", "tensor_scalar", accP[:, :],
                                        buf[:, :], gate_b[:, 0:1], None,
                                        ALU.mult, sem=sv)
                else:
                    comb_done[k] = p.op("vector", "scalar_tensor_tensor",
                                        accP[:, :], buf[:, :],
                                        gate_b[:, n:n + 1], accP[:, :],
                                        ALU.mult, ALU.add, sem=sv)
                p.wait("vector", sv, comb_done[k])
                if n == 15:
                    if ph == 1:
                        p.wait("vector", qts, tmp_store[0])
                    ev = p.op("vector", "tensor_copy", lst_b[2][:, :], accP[:, :],
                              sem=sv)
                    p.wait("scalar", sv, ev)
                    if ph == 0:
                        dst = s_tmpa.rearrange("(p a) r -> p (a r)", p=128)
                    else:
                        dst = s_tmpb.rearrange("r (a c) -> (r a) c", a=4)
                    tmp_store[ph] = p.dma("scalar", qts, dst, lst_b[2][:, :])
                    if ph == 1:
                        psT_guard[0] = psT_guard[1] = (sv, ev)

        if want("A3"):
            p.wait("sync", te, a2_done_t)
            p.wait("sync", qsw[0], p_st)
            wqe3o = W64.rearrange("p (kc n) -> p kc n", n=KV)
            p.dma("sync", qw, wqe3o, s_wqe.rearrange("kc p n -> p kc n"))
            wqe_loaded = qw.n
            qev = {}
            first_q = True
            last_mm = None
            for s in range(2):
                if s > 0:
                    p.wait("sync", te, last_mm)
                p.dma("sync", qxn[0], xnbuf3, s_xn[s].rearrange("kc p t -> p kc t"))
                ld = qxn[0].n
                p.wait("tensor", qxn[0], ld)
                p.wait("tensor", qw, wqe_loaded)
                if first_q:
                    p.wait("tensor", sv, a2_done_v)
                    first_q = False
                for j in range(NH):
                    emit_comb_steps(2)
                    bank = j % 2
                    g = qev.get((s, j - 2)) if j >= 2 else qev.get((s - 1, j + 6))
                    if g is not None:
                        p.wait("tensor", ss, g)
                    mm = None
                    for kc in range(NKC):
                        mm = p.op("tensor", "matmul",
                                  psA[:, bank * 512:(bank + 1) * 512],
                                  W64[:, kc * KV + j * 128: kc * KV + (j + 1) * 128],
                                  xnbuf[:, kc * SCT:(kc + 1) * SCT],
                                  start=(kc == 0), stop=(kc == NKC - 1), sem=te)
                    last_mm = mm
                    p.wait("scalar", te, mm)
                    qev[(s, j)] = p.op("scalar", "activation",
                                       Q_fm[:, j * TOWN + s * SCT: j * TOWN + (s + 1) * SCT],
                                       psA[:, bank * 512:(bank + 1) * 512],
                                       AF.Copy, scale=INVSQ, sem=ss)
            emit_comb_steps(32)
            a3_done_t = te.n
            a3_done_s = ss.n
            if "Q" in tap:
                p.wait("sync", ss, a3_done_s)
                p.dma("sync", qt_, tap["Q"], Q_fm[:, :])

        if want("B"):
            p.wait("scalar", ql[0], lst_ld[30])
            p.wait("scalar", ql[1], lst_ld[31])
            p.wait("scalar", qts, tmp_store[1])
            p.wait("gpsimd", te, a3_done_t)
            wo3o = W64.rearrange("p (kc n) -> p kc n", n=D)
            wo3i = wo.rearrange("(kc p) n -> p kc n", p=128)
            for hv in range(2):
                p.dma("gpsimd", qw, wo3o[:, hv * 4:(hv + 1) * 4], wo3i[:, hv * 4:(hv + 1) * 4])
            wo_loaded = qw.n

            mx = stats[:, 8:9]
            nmx = stats[:, 9:10]
            sexp = stats[:, 10:11]
            rs = stats[:, 11:12]

            exp_done = {"last": a3_done_s}
            o_ev = {"last": None}
            wo_guard = [None, None]
            woev = {}
            xres_ld = {}
            ln2_done = {}

            def b_scores(qt, h):
                p.wait("tensor", ss, exp_done["last"])
                mm = None
                for n in range(4):
                    mm = p.op("tensor", "matmul",
                              psA[:, n * 512:(n + 1) * 512],
                              Q_fm[:, h * TOWN + qt * 128: h * TOWN + (qt + 1) * 128],
                              K_fm[:, h * TFULL + n * 512: h * TFULL + (n + 1) * 512],
                              start=True, stop=True, sem=te)
                return mm

            def b_softmax(qt, h, sc_done, pbuf):
                p.wait("vector", te, sc_done)
                v = p.op("vector", "tensor_reduce", nmx, psA[:, :], AX.X, ALU.max,
                         negate=True, sem=sv)
                p.wait("scalar", sv, v)
                e = p.op("scalar", "activation", pbuf[:, :], psA[:, :], AF.Exp,
                         bias=nmx, sem=ss, accum_out=sexp)
                exp_done["last"] = e
                p.wait("vector", ss, e)
                rv = p.op("vector", "reciprocal", rs, sexp, sem=sv)
                p.wait("scalar", sv, rv)
                return p.op("scalar", "activation", pbuf[:, :], pbuf[:, :], AF.Identity,
                            scale=rs, sem=ss)

            def b_ptrans_and_o(h, ps_done, pbuf):
                mm = None
                for g in range(2):
                    bank = g % 2
                    p.wait("tensor", ss, ps_done)
                    wait_psT("tensor", bank)
                    tv = None
                    for i in range(8):
                        blk = g * 8 + i
                        tv = p.op("tensor", "transpose",
                                  psT[:, bank * 1024 + i * 128: bank * 1024 + (i + 1) * 128],
                                  pbuf[:, blk * 128:(blk + 1) * 128], idn[:, :], sem=te)
                    p.wait("vector", te, tv)
                    dst = PT_sb.rearrange("p (b d) -> p b d", d=128)[:, g * 8:(g + 1) * 8]
                    psT_guard[bank] = (sv, p.op("vector", "tensor_copy", dst, psT4[:, bank],
                                                sem=sv))
                wait_psT("tensor", 1)
                if o_ev["last"] is not None:
                    p.wait("tensor", ss, o_ev["last"])
                if wo_guard[1] is not None:
                    p.wait("tensor", sv, wo_guard[1])
                for blk in range(16):
                    mm = p.op("tensor", "matmul", psO[:, 0:128],
                              V_tm[:, blk * KV + h * 128: blk * KV + (h + 1) * 128],
                              PT_sb[:, blk * 128:(blk + 1) * 128],
                              start=(blk == 0), stop=(blk == 15), sem=te)
                p.wait("scalar", te, mm)
                o_ev["last"] = p.op("scalar", "activation", O_fm[:, h * 128:(h + 1) * 128],
                                    psO[:, 0:128], AF.Copy, sem=ss)

            xnew_st = {}
            ln2_trans = {}

            def emit_xres_load(qt):
                if qt >= 1:
                    p.wait("sync", sv, woev[qt - 1])
                    p.wait("sync", ss, ln2_done[qt - 1])
                    p.wait("sync", qt_, xnew_st[qt - 1])
                    p.wait("sync", te, ln2_trans[qt - 1])
                p.dma("sync", qs, xresA[:, :], x_own[qt * 128:(qt + 1) * 128, 0:2048])
                p.dma("sync", qs, xresB[:, :], x_own[qt * 128:(qt + 1) * 128, 2048:4096])
                xres_ld[qt] = qs.n

            def emit_wo(qt):
                p.wait("tensor", ss, o_ev["last"])
                p.wait("tensor", qw, wo_loaded)
                wops = [psW, psO]
                for cb in range(8):
                    bank = wops[cb % 2]
                    if wo_guard[cb % 2] is not None:
                        p.wait("tensor", sv, wo_guard[cb % 2])
                    mm = None
                    for kc in range(NH):
                        mm = p.op("tensor", "matmul", bank[:, 0:512],
                                  O_fm[:, kc * 128:(kc + 1) * 128],
                                  W64[:, kc * D + cb * 512: kc * D + (cb + 1) * 512],
                                  start=(kc == 0), stop=(kc == NH - 1), sem=te)
                    p.wait("vector", te, mm)
                    if cb == 0:
                        p.wait("vector", qs, xres_ld[qt])
                    xr = xresA if cb < 4 else xresB
                    co = (cb % 4) * 512
                    wo_guard[cb % 2] = p.op("vector", "tensor_add",
                                            xr[:, co:co + 512], xr[:, co:co + 512],
                                            bank[:, 0:512], sem=sv)
                woev[qt] = wo_guard[1]
                p.wait("sync", sv, woev[qt])
                p.dma("sync", qt_, s_xnew[qt * 128:(qt + 1) * 128, 0:2048], xresA[:, :])
                xnew_st[qt] = p.dma("sync", qt_, s_xnew[qt * 128:(qt + 1) * 128, 2048:4096],
                                    xresB[:, :])

            def emit_ln2_stats(qt):
                p.wait("scalar", sv, woev[qt])
                p.wait("scalar", qt_, xnew_st[qt])
                if qt >= 1:
                    p.wait("scalar", te, ln2_trans[qt - 1])
                ln2_done[qt] = ln_chunk([(xresA[:, :], xn2dA[:, :]),
                                         (xresB[:, :], xn2dB[:, :])])

            def emit_ln2_trans(qt):
                s2, c2 = divmod(qt, 4)
                dumps = [xn2dA, xn2dB]
                for gg in range(4):
                    bank = gg % 2
                    p.wait("tensor", ss, ln2_done[qt])
                    if c2 == 0 and s2 >= 1 and gg == 0:
                        p.wait("vector", qt_, xn2_store[s2 - 1])
                    wait_psT("tensor", bank)
                    tv = None
                    for i in range(8):
                        kc = gg * 8 + i
                        src = dumps[kc // 16]
                        tv = p.op("tensor", "transpose",
                                  psT[:, bank * 1024 + i * 128: bank * 1024 + (i + 1) * 128],
                                  src[:, (kc % 16) * 128:(kc % 16 + 1) * 128],
                                  idn[:, :], sem=te)
                    p.wait("vector", te, tv)
                    dst = xnbuf3[:, gg * 8:(gg + 1) * 8, c2 * 128:(c2 + 1) * 128]
                    psT_guard[bank] = (sv, p.op("vector", "tensor_copy", dst, psT4[:, bank],
                                                sem=sv))
                ln2_trans[qt] = te.n
                if c2 == 3:
                    p.wait("sync", sv)
                    xn2_store[s2] = p.dma("sync", qt_,
                                          s_xn2[s2].rearrange("kc p t -> p kc t"), xnbuf3)

            emit_xres_load(0)
            pend = None
            for qt in range(NQT):
                for h in range(NH):
                    sc = b_scores(qt, h)
                    if qt >= 1 and h == 1:
                        emit_wo(qt - 1)
                    if pend is not None:
                        b_ptrans_and_o(*pend)
                    pend_ps = b_softmax(qt, h, sc, P_tm[h % 2])
                    if qt == 0 and h == 0 and "P0" in tap:
                        p.wait("sync", ss, pend_ps)
                        p.dma("sync", qt_, tap["P0"], P_tm[0][:, :])
                        p.wait("scalar", qt_)
                    pend = (h, pend_ps, P_tm[h % 2])
                    if qt >= 1:
                        if h == 2:
                            emit_ln2_stats(qt - 1)
                        elif h == 5:
                            emit_ln2_trans(qt - 1)
                        elif h == 6:
                            emit_xres_load(qt)
            b_ptrans_and_o(*pend)
            emit_wo(NQT - 1)
            emit_ln2_stats(NQT - 1)
            emit_ln2_trans(NQT - 1)
            b_done_t = te.n
            b_done_v = sv.n
            b_done_s = ss.n
            if "XNEW" in tap:
                p.wait("sync", qt_)
                p.dma("sync", qt_, tap["XNEW"], s_xnew)
            if "XN2" in tap:
                p.wait("sync", qt_)
                for s2 in range(2):
                    p.dma("sync", qt_, tap["XN2"][s2], s_xn2[s2])

        if want("C"):
            xnew_stores = qt_.n

            k3o = ker_bf.rearrange("p (kc n) -> p kc n", n=2048)
            q_loaded = {}
            ker_free = {}

            def emit_ker_quarter(hf, q):
                if hf == 0 and q == 0:
                    p.wait("gpsimd", te, b_done_t)
                    p.wait("gpsimd", ss, b_done_s)
                    p.wait("gpsimd", sv, b_done_v)
                    p.wait("gpsimd", qt_, xnew_stores)
                if hf == 1:
                    p.wait("gpsimd", te, ker_free[q])
                k3i = ker.rearrange("(kc p) n -> p kc n", p=128)[:, :, hf * 2048:(hf + 1) * 2048]
                q_loaded[(hf, q)] = p.dma("gpsimd", qk[q], k3o[:, q * 8:(q + 1) * 8],
                                          k3i[:, q * 8:(q + 1) * 8])

            for q in range(4):
                emit_ker_quarter(0, q)
            p.wait("gpsimd", qts, tmp_store[1])
            acat3 = A_catT.rearrange("p (kc r) -> p kc r", r=R)
            p.dma("gpsimd", ql[0], acat3[:, :, 0:32], s_tmpa.rearrange("(kc p) r -> p kc r", p=128))
            p.dma("gpsimd", ql[0], acat3[:, :, 32:64], last.rearrange("(kc p) r -> p kc r", p=128))
            p.dma("gpsimd", ql[0], B_catT[0:32, :], s_tmpb)
            p.dma("gpsimd", ql[0], B_catT[32:64, :], lbst)
            lora_ready = ql[0].n

            woev2 = {}
            kermm_done = None
            ax_ready = {}
            loaded_s2 = [None]
            idx = 0
            for hf in range(2):
                if hf == 0:
                    p.wait("sync", te, b_done_t)
                    p.wait("sync", ss, b_done_s)
                    p.wait("sync", sv, b_done_v)
                    p.wait("sync", qt_, xnew_stores)
                p.dma("sync", qsw[hf], bias_h[:, :], bias_b[:, hf * 2048:(hf + 1) * 2048])
                bias_ld = qsw[hf].n
                for s2 in ((0, 1) if hf == 0 else (1, 0)):
                    if loaded_s2[0] != s2:
                        if (hf, s2) != (0, 0):
                            p.wait("sync", te, kermm_done)
                        if hf == 0 and want("B"):
                            p.wait("sync", qt_, xn2_store[s2])
                        p.dma("sync", qx2, xnbuf3, s_xn2[s2].rearrange("kc p t -> p kc t"))
                        xn2_ld = qx2.n
                        loaded_s2[0] = s2
                    for t in range(4):
                        chunk = s2 * 4 + t
                        g = woev2.get(idx - 2)
                        if g is not None:
                            p.wait("sync", sv, g)
                        p.dma("sync", qb[idx % 2], xnb[idx % 2][:, :],
                              s_xnew[chunk * 128:(chunk + 1) * 128, hf * 2048:(hf + 1) * 2048])
                        xnew_ld = qb[idx % 2].n
                        p.wait("tensor", qx2, xn2_ld)
                        g = woev2.get(idx - 1)
                        if g is not None:
                            p.wait("tensor", sv, g)
                        if idx == 0:
                            p.wait("tensor", ss, b_done_s)
                        mm = None
                        for kc in range(NKC):
                            if kc % 8 == 0:
                                p.wait("tensor", qk[kc // 8], q_loaded[(hf, kc // 8)])
                            lhs = xnbuf[:, kc * SCT + t * 128: kc * SCT + (t + 1) * 128]
                            for cb in range(4):
                                mm = p.op("tensor", "matmul", psA[:, cb * 512:(cb + 1) * 512],
                                          lhs,
                                          ker_bf[:, kc * 2048 + cb * 512: kc * 2048 + (cb + 1) * 512],
                                          start=(kc == 0), stop=False, sem=te)
                            if hf == 0 and idx == 7 and kc % 8 == 7:
                                ker_free[kc // 8] = te.n
                                emit_ker_quarter(1, kc // 8)
                        if hf == 0 and t == 0:
                            p.wait("tensor", ql[0], lora_ready)
                            if s2 == 0:
                                p.wait("tensor", sv, b_done_v)
                            else:
                                p.wait("tensor", ss, ax_ready[1 - s2])
                            axm = None
                            for kc in range(NKC):
                                axm = p.op("tensor", "matmul", psW[0:64, 0:512],
                                           A_catT[:, kc * R:(kc + 1) * R],
                                           xnbuf[:, kc * SCT:(kc + 1) * SCT],
                                           start=(kc == 0), stop=(kc == NKC - 1), sem=te)
                            p.wait("scalar", te, axm)
                            if s2 == 0:
                                p.wait("scalar", qt_, xnew_stores)
                            ax_ready[s2] = p.op("scalar", "activation",
                                                Ax_bf[:, s2 * SCT:(s2 + 1) * SCT],
                                                psW[0:64, 0:512], AF.Copy, sem=ss)
                        p.wait("tensor", ss, ax_ready[s2])
                        for cb in range(4):
                            mm = p.op("tensor", "matmul", psA[:, cb * 512:(cb + 1) * 512],
                                      Ax_bf[:, chunk * 128:(chunk + 1) * 128],
                                      B_catT[:, hf * 2048 + cb * 512: hf * 2048 + (cb + 1) * 512],
                                      start=False, stop=True, sem=te)
                        kermm_done = mm
                        p.wait("vector", te, kermm_done)
                        p.wait("vector", qb[idx % 2], xnew_ld)
                        p.wait("vector", qsw[hf], bias_ld)
                        for cb in range(4):
                            p.op("vector", "tensor_add", xnb[idx % 2][:, cb * 512:(cb + 1) * 512],
                                 xnb[idx % 2][:, cb * 512:(cb + 1) * 512],
                                 psA[:, cb * 512:(cb + 1) * 512])
                        woev2[idx] = p.op("vector", "tensor_add", xnb[idx % 2][:, :],
                                          xnb[idx % 2][:, :], bias_h[:, :], sem=sv)
                        p.wait("sync", sv, woev2[idx])
                        p.dma("sync", qt_, out[chunk * 128:(chunk + 1) * 128,
                                              hf * 2048:(hf + 1) * 2048],
                              xnb[idx % 2][:, :])
                        idx += 1

        p.wait("sync", qt_)

        with nc.Block() as block:
            @block.sync
            def _(e):
                for f in p.q["sync"]:
                    f(e)

            @block.scalar
            def _(e):
                for f in p.q["scalar"]:
                    f(e)

            @block.vector
            def _(e):
                for f in p.q["vector"]:
                    f(e)

            @block.tensor
            def _(e):
                for f in p.q["tensor"]:
                    f(e)

            @block.gpsimd
            def _(e):
                for f in p.q["gpsimd"]:
                    f(e)

    return nc


def make_in_maps(inputs):
    X = np.asarray(inputs["L0_attn_x"], dtype=np.float32)
    wq = np.ascontiguousarray(np.asarray(inputs["L0_attn_wq"], np.float32))
    wk = np.ascontiguousarray(np.asarray(inputs["L0_attn_wk"], np.float32))
    wv = np.ascontiguousarray(np.asarray(inputs["L0_attn_wv"], np.float32))
    wo = np.ascontiguousarray(np.asarray(inputs["L0_attn_wo"], np.float32))
    gate = np.asarray(inputs["L0_ffn_gate"], np.float32)
    ker = np.ascontiguousarray(np.asarray(inputs["L0_ffn_kernel"], np.float32))
    bias = np.asarray(inputs["L0_ffn_bias"], np.float32)
    lA = np.asarray(inputs["L0_ffn_lora_A"], np.float32)
    lB = np.asarray(inputs["L0_ffn_lora_B"], np.float32)
    lAs = np.asarray(inputs["L0_ffn_lora_A_shared"], np.float32)
    lBs = np.asarray(inputs["L0_ffn_lora_B_shared"], np.float32)

    lat = np.ascontiguousarray(lA.transpose(0, 2, 1)).reshape(16, 128, 1024)
    lbt = np.ascontiguousarray(lB.transpose(0, 2, 1)).reshape(16, 128, 1024)
    last = np.ascontiguousarray(lAs.T)
    lbst = np.ascontiguousarray(lBs.T)
    bias_b = np.ascontiguousarray(np.broadcast_to(bias, (128, 4096)))

    in_maps = []
    for c in range(8):
        b, h = divmod(c, 2)
        gate_b = np.ascontiguousarray(np.broadcast_to(gate[2 * b + h], (128, 16)))
        in_maps.append({
            "x_own": np.ascontiguousarray(X[b, h * 1024:(h + 1) * 1024]),
            "x_oth": np.ascontiguousarray(X[b, (1 - h) * 1024:(2 - h) * 1024]),
            "wq": wq, "wk": wk, "wv": wv, "wo": wo, "ker": ker,
            "bias_b": bias_b, "gate_b": gate_b,
            "lat": lat, "lbt": lbt, "last": last, "lbst": lbst,
        })
    return in_maps


def kernel(**inputs):
    nc = build_nc()
    in_maps = make_in_maps(inputs)
    res = run_bass_kernel_spmd(nc, in_maps, core_ids=list(range(8)))
    out = np.empty((4, 2048, 4096), np.float32)
    for c, r in enumerate(res.results):
        b, h = divmod(c, 2)
        out[b, h * 1024:(h + 1) * 1024] = r["out"]
    return out


# version 18
# speedup vs baseline: 1.0056x; 1.0056x over previous
import contextlib
import numpy as np
import ml_dtypes
import concourse.bass as bass
import concourse.mybir as mybir
from concourse.bass_utils import run_bass_kernel_spmd

F32 = mybir.dt.float32
BF16 = mybir.dt.bfloat16
AF = mybir.ActivationFunctionType
ALU = mybir.AluOpType
AX = mybir.AxisListType

D = 4096
KV = 1024
NH = 8
NKC = 32
TOWN = 1024
TFULL = 2048
NSC = 4
SCT = 512
NQT = 8
R = 64
INVSQ = float(1.0 / np.sqrt(128.0))
EPS = 1e-5

SBUF_BASE = 16512
KiB = 1024

ENGINES = ("sync", "scalar", "vector", "tensor", "gpsimd")


class Sem:
    __slots__ = ("h", "n")

    def __init__(self, h):
        self.h = h
        self.n = 0


class P:

    def __init__(self, nc):
        self.nc = nc
        self.q = {e: [] for e in ENGINES}

    def emit(self, eng, fn):
        self.q[eng].append(fn)

    def dma(self, eng, sem, out, in_):
        sem.n += 16
        self.emit(eng, lambda e, o=out, i=in_, h=sem.h: e.dma_start(out=o, in_=i).then_inc(h, 16))
        return sem.n

    def wait(self, eng, sem, val=None):
        v = sem.n if val is None else val
        if v is None or v <= 0:
            return
        self.emit(eng, lambda e, h=sem.h, v=v: e.wait_ge(h, v))

    def op(self, eng, name, *args, sem=None, **kw):
        if sem is not None:
            sem.n += 1
        s = sem

        def f(e, name=name, args=args, kw=kw, s=s):
            ins = getattr(e, name)(*args, **kw)
            if s is not None:
                ins.then_inc(s.h, 1)
        self.emit(eng, f)
        return None if s is None else s.n


def build_nc(taps=(), upto="C"):
    nc = bass.Bass("TRN2", target_bir_lowering=False)
    phase_order = ["P", "A1", "A2", "A3", "B", "C"]
    upto_i = phase_order.index(upto)

    def want(ph):
        return phase_order.index(ph) <= upto_i

    x_own = nc.dram_tensor("x_own", [TOWN, D], F32, kind="ExternalInput").ap()
    x_oth = nc.dram_tensor("x_oth", [TOWN, D], F32, kind="ExternalInput").ap()
    wq = nc.dram_tensor("wq", [D, D], F32, kind="ExternalInput").ap()
    wk = nc.dram_tensor("wk", [D, KV], F32, kind="ExternalInput").ap()
    wv = nc.dram_tensor("wv", [D, KV], F32, kind="ExternalInput").ap()
    wo = nc.dram_tensor("wo", [KV, D], F32, kind="ExternalInput").ap()
    ker = nc.dram_tensor("ker", [D, D], F32, kind="ExternalInput").ap()
    bias_b = nc.dram_tensor("bias_b", [128, D], F32, kind="ExternalInput").ap()
    gate_bi = nc.dram_tensor("gate_b", [128, 16], F32, kind="ExternalInput").ap()
    lat = nc.dram_tensor("lat", [16, 128, 1024], F32, kind="ExternalInput").ap()
    lbt = nc.dram_tensor("lbt", [16, 128, 1024], F32, kind="ExternalInput").ap()
    last = nc.dram_tensor("last", [D, 32], F32, kind="ExternalInput").ap()
    lbst = nc.dram_tensor("lbst", [32, D], F32, kind="ExternalInput").ap()

    out = nc.dram_tensor("out", [TOWN, D], F32, kind="ExternalOutput").ap()

    s_wqe = nc.dram_tensor("s_wqe", [NKC, 128, KV], BF16, kind="Internal").ap()
    s_xn = nc.dram_tensor("s_xn", [NSC, NKC, 128, SCT], BF16, kind="Internal").ap()
    s_xn2 = nc.dram_tensor("s_xn2", [2, NKC, 128, SCT], BF16, kind="Internal").ap()
    s_xnew = nc.dram_tensor("s_xnew", [TOWN, D], F32, kind="Internal").ap()
    s_tmpa = nc.dram_tensor("s_tmpa", [D, 32], BF16, kind="Internal").ap()
    s_tmpb = nc.dram_tensor("s_tmpb", [32, D], BF16, kind="Internal").ap()

    ident = nc.inline_tensor(np.eye(128, dtype=ml_dtypes.bfloat16), name="ident").ap()

    tap = {}
    for name, shape, dt in [
        ("K", [128, NH * TFULL], BF16), ("V", [128, 16 * KV], BF16),
        ("Q", [128, NH * TOWN], BF16), ("XN", [NSC, NKC, 128, SCT], BF16),
        ("XNEW", [TOWN, D], F32), ("XN2", [2, NKC, 128, SCT], BF16),
        ("WQE", [NKC, 128, KV], BF16),
        ("P0", [128, TFULL], BF16), ("O0", [128, NH * 128], BF16),
        ("XR0", [128, D], F32), ("PT0", [128, 16 * 128], BF16),
    ]:
        if name in taps:
            tap[name] = nc.dram_tensor("tap_" + name, shape, dt, kind="ExternalOutput").ap()

    def sb(name, shape, dt, off_kib):
        return nc.alloc_sbuf_tensor_at(name, shape, dt, offset=SBUF_BASE + int(off_kib * KiB)).ap()

    xnbuf = sb("xnbuf", [128, NKC * SCT], BF16, 0)
    V_tm = sb("V_tm", [128, 16 * KV], BF16, 32)
    xbuf = [sb("xbuf0", [128, D], F32, 32),
            sb("xbuf1", [128, D], F32, 48)]
    xn_tm = sb("xn_tm", [128, D], BF16, 64)
    xresA = sb("xresA", [128, 2048], F32, 64)
    Q_fm = sb("Q_fm", [128, NH * TOWN], BF16, 72)
    xnsA = sb("xnsA", [128, 16 * SCT], BF16, 72)
    xnsB = sb("xnsB", [128, 8 * SCT], BF16, 88)
    P_tm = [sb("P_tm0", [128, TFULL], BF16, 88),
            sb("P_tm1", [128, TFULL], BF16, 92)]
    xn2dA = sb("xn2dA", [128, 2048], BF16, 64)
    xn2dB = sb("xn2dB", [128, 2048], BF16, 192)
    W64 = sb("W64", [128, NKC * KV], BF16, 96)
    K_fm = sb("K_fm", [128, NH * TFULL], BF16, 160)
    xnsC = sb("xnsC", [128, 8 * SCT], BF16, 192)
    xresB = sb("xresB", [128, 2048], F32, 192)
    wqb = [sb("wqb0", [128, D], BF16, 72), sb("wqb1", [128, D], BF16, 80)]
    ptmpb = sb("ptmpb", [128, KV], BF16, 88)
    wqec = [sb("wqec0", [128, KV], BF16, 90), sb("wqec1", [128, KV], BF16, 92)]
    wqecP = sb("wqecP", [128, 2 * KV], BF16, 90)
    ker_bf = sb("ker_bf", [128, NKC * 2048], BF16, 32)
    xnb = [sb("xnb0", [128, 2048], F32, 160), sb("xnb1", [128, 2048], F32, 168)]
    bias_h = sb("bias_h", [128, 2048], F32, 176)
    A_catT = sb("A_catT", [128, NKC * R], BF16, 184)
    B_catT = sb("B_catT", [R, D], BF16, 188)
    Ax_bf = sb("Ax_bf", [R, TOWN], BF16, 196)
    idn = sb("idn", [128, 128], BF16, 200)
    gate_b = sb("gate_sb", [128, 16], F32, 200.25)
    stats = sb("stats", [128, 64], F32, 200.5)
    ceps = sb("ceps", [128, 1], F32, 200.40625)
    O_fm = sb("O_fm", [128, NH * 128], BF16, 201)
    PT_sb = sb("PT_sb", [128, 16 * 128], BF16, 203)
    lst_b = [sb(f"lstb{i}", [128, KV], BF16, 88 + 2 * i) for i in range(3)]

    psA = nc.alloc_psum_tensor("psA", [128, 4 * 512], F32).ap()
    psT = nc.alloc_psum_tensor("psT", [128, 2048], BF16).ap()
    psO = nc.alloc_psum_tensor("psO", [128, 512], F32).ap()
    psW = nc.alloc_psum_tensor("psW", [128, 512], F32).ap()

    p = P(nc)
    xnbuf3 = xnbuf.rearrange("p (kc t) -> p kc t", t=SCT)
    psT4 = psT.rearrange("p (b i d) -> p b i d", b=2, d=128)

    with contextlib.ExitStack() as stx:
        mk = lambda nm: Sem(stx.enter_context(nc.semaphore(nm)))
        qs = mk("qs")
        qt_ = mk("qt")
        qg = mk("qg")
        sv = mk("sv")
        ss = mk("ss")
        te = mk("te")
        qw = mk("qw")
        qp = [mk("qp0"), mk("qp1")]
        qsw = [mk("qsw0"), mk("qsw1")]
        qxn = [mk("qxn0"), mk("qxn1")]
        qk = [mk(f"qk{i}") for i in range(4)]
        ql = [mk("ql0"), mk("ql1")]
        qts = mk("qts")
        qb = [mk("qb0"), mk("qb1")]
        qx2 = mk("qx2")
        qx = [mk("qx0"), mk("qx1")]

        psT_guard = [None, None]

        p.dma("sync", qs, idn[:, :], ident)
        p.dma("sync", qs, gate_b[:, :], gate_bi)
        eps_set = p.op("vector", "memset", ceps[:, :], EPS, sem=sv)
        p.wait("scalar", sv, eps_set)

        a1_done_v = a1_done_t = a2_done_v = a2_done_t = None
        a3_done_t = a3_done_s = b_done_t = b_done_v = b_done_s = None
        xn_store = {}
        xn2_store = {}

        presum_done = {}
        wqe_load = {}
        wqe_stored = {}

        def emit_presum_load(kc):
            buf = wqb[kc % 2]
            if kc >= 2:
                p.wait("gpsimd", sv, presum_done[kc - 2])
            wqe_load[kc] = p.dma("gpsimd", qp[kc % 2], buf[:, :], wq[kc * 128:(kc + 1) * 128, :])

        def emit_presum_compute(kc):
            buf = wqb[kc % 2]
            p.wait("vector", qp[kc % 2], wqe_load[kc])
            v4 = buf.rearrange("p (h g d) -> p g h d", g=4, d=128)
            pt3 = ptmpb.rearrange("p (h d) -> p h d", d=128)
            p.op("vector", "tensor_add", pt3, v4[:, 0], v4[:, 1])
            p.op("vector", "tensor_add", pt3, pt3, v4[:, 2])
            wc = wqec[kc % 2]
            if kc >= 2:
                g = wqe_stored.get(kc - 1 if kc % 2 == 0 else kc - 2)
                if g is not None:
                    p.wait("vector", qsw[0], g)
            presum_done[kc] = p.op("vector", "tensor_add",
                                   wc.rearrange("p (h d) -> p h d", d=128),
                                   pt3, v4[:, 3], sem=sv)
            if kc % 2 == 1:
                p.wait("sync", sv, presum_done[kc])
                wqe_stored[kc] = p.dma(
                    "sync", qsw[0],
                    s_wqe[kc - 1:kc + 1].rearrange("two p n -> p two n"),
                    wqecP.rearrange("p (two n) -> p two n", two=2))

        def ln_chunk(pieces):
            mean, var, tvar, tsd, rstd, negmr = (stats[:, i:i + 1] for i in range(6))
            mv = stats[:, 0:2]
            k = 0
            bl = None
            for src, dump in pieces:
                w = (D // len(pieces)) // 512
                for t0 in range(w):
                    bl = p.op("vector", "bn_stats", stats[:, 16 + 6 * k:22 + 6 * k],
                              src[:, t0 * 512:(t0 + 1) * 512], sem=sv)
                    k += 1
            p.wait("vector", sv, bl)
            ag = p.op("vector", "bn_aggr", mv, stats[:, 16:16 + 6 * k], sem=sv)
            p.wait("vector", sv, ag)
            v2 = p.op("vector", "tensor_scalar", tvar, var, 1.0, EPS,
                      ALU.mult, ALU.add, sem=sv)
            p.wait("scalar", sv, v2)
            s3 = p.op("scalar", "activation", tsd, tvar, AF.Sqrt, sem=ss)
            p.wait("vector", ss, s3)
            rv = p.op("vector", "reciprocal", rstd, tsd, sem=sv)
            p.wait("vector", sv, rv)
            v3 = p.op("vector", "tensor_scalar", negmr, mean, rstd, -1.0,
                      ALU.mult, ALU.mult, sem=sv)
            p.wait("scalar", sv, v3)
            r = None
            for src, dump in pieces:
                r = p.op("scalar", "activation", dump, src, AF.Identity,
                         scale=rstd, bias=negmr, sem=ss)
            return r

        def transpose_chunk(dump_bf, dst3, c, norm_done):
            tv = ev = None
            for g in range(4):
                bank = g % 2
                p.wait("tensor", ss, norm_done)
                if psT_guard[bank] is not None:
                    p.wait("tensor", sv, psT_guard[bank])
                for i in range(8):
                    kc = g * 8 + i
                    tv = p.op("tensor", "transpose",
                              psT[:, bank * 1024 + i * 128: bank * 1024 + (i + 1) * 128],
                              dump_bf[:, kc * 128:(kc + 1) * 128], idn[:, :], sem=te)
                p.wait("vector", te, tv)
                dst = dst3[:, g * 8:(g + 1) * 8, c * 128:(c + 1) * 128]
                ev = p.op("vector", "tensor_copy", dst, psT4[:, bank], sem=sv)
                psT_guard[bank] = ev
            return tv, ev

        if want("A1"):
            emit_presum_load(0)
            emit_presum_load(1)
            wk3o = W64.rearrange("p (kc n) -> p kc n", n=KV)
            wk3i = wk.rearrange("(kc p) n -> p kc n", p=128)
            for qtr in range(4):
                p.dma("gpsimd", qw, wk3o[:, qtr * 8:(qtr + 1) * 8], wk3i[:, qtr * 8:(qtr + 1) * 8])
            wk_loaded = qw.n

            kev = {}
            kmm_done = {}
            ln_done = {}
            trans_done = {}
            for i in range(16):
                s, c = divmod(i, 4)
                src = x_own if i < 8 else x_oth
                row = (i % 8) * 128
                if i >= 2:
                    p.wait("scalar", ss, ln_done[i - 2])
                xld = p.dma("scalar", qx[i % 2], xbuf[i % 2][:, :], src[row:row + 128, :])
                p.wait("scalar", qx[i % 2], xld)
                p.wait("vector", qx[i % 2], xld)
                if i >= 1:
                    p.wait("scalar", te, trans_done[i - 1])
                ln_done[i] = ln_chunk([(xbuf[i % 2][:, :], xn_tm[:, :])])
                if i > 0:
                    emit_presum_compute(2 * (i - 1))
                    emit_presum_compute(2 * (i - 1) + 1)
                    emit_presum_load(2 * i)
                    emit_presum_load(2 * i + 1)
                if c == 0 and s >= 1:
                    p.wait("vector", te, kmm_done[s - 1])
                    p.wait("vector", qt_, xn_store[s - 1])
                trans_done[i], _ = transpose_chunk(xn_tm[:, :], xnbuf3, c, ln_done[i])
                if c == 3:
                    p.wait("sync", sv)
                    xn_store[s] = p.dma("sync", qt_, s_xn[s].rearrange("kc p t -> p kc t"),
                                        xnbuf3)
                    p.wait("tensor", sv)
                    p.wait("tensor", qw, wk_loaded)
                    for j in range(NH):
                        bank = j % 2
                        if j >= 2 and kev.get(j - 2) is not None:
                            p.wait("tensor", sv, kev[j - 2])
                        mm = None
                        for kc in range(NKC):
                            mm = p.op("tensor", "matmul",
                                      psA[:, bank * 512:(bank + 1) * 512],
                                      W64[:, kc * KV + j * 128: kc * KV + (j + 1) * 128],
                                      xnbuf[:, kc * SCT:(kc + 1) * SCT],
                                      start=(kc == 0), stop=(kc == NKC - 1), sem=te)
                        kmm_done[s] = mm
                        p.wait("vector", te, mm)
                        kev[j] = p.op("vector", "tensor_copy",
                                      K_fm[:, j * TFULL + s * SCT: j * TFULL + (s + 1) * SCT],
                                      psA[:, bank * 512:(bank + 1) * 512], sem=sv)
            emit_presum_compute(NKC - 2)
            emit_presum_compute(NKC - 1)
            a1_done_v = sv.n
            a1_done_t = te.n
            p_done_v = presum_done[NKC - 1]
            p_st = wqe_stored[NKC - 1]
            if "WQE" in tap:
                p.wait("sync", qsw[0], p_st)
                for kc in range(NKC):
                    p.dma("sync", qt_, tap["WQE"][kc], s_wqe[kc])
            if "XN" in tap:
                p.wait("sync", qt_)
                for s in range(NSC):
                    p.dma("sync", qt_, tap["XN"][s], s_xn[s])
            if "K" in tap:
                p.wait("sync", sv, a1_done_v)
                p.dma("sync", qt_, tap["K"], K_fm[:, :])

        if want("A2"):
            p.wait("gpsimd", te, a1_done_t)
            wv3o = W64.rearrange("p (kc n) -> p kc n", n=KV)
            wv3i = wv.rearrange("(kc p) n -> p kc n", p=128)
            for qtr in range(4):
                p.dma("gpsimd", qw, wv3o[:, qtr * 8:(qtr + 1) * 8], wv3i[:, qtr * 8:(qtr + 1) * 8])
            wv_loaded = qw.n

            def xn_view(par, kc):
                if par == 0:
                    return xnbuf[:, kc * SCT:(kc + 1) * SCT]
                if kc < 16:
                    return xnsA[:, kc * SCT:(kc + 1) * SCT]
                if kc < 24:
                    return xnsB[:, (kc - 16) * SCT:(kc - 15) * SCT]
                return xnsC[:, (kc - 24) * SCT:(kc - 23) * SCT]

            def xn_load(par, s, guard_te):
                p.wait("sync", qt_, xn_store[s])
                if guard_te is not None:
                    p.wait("sync", te, guard_te)
                if par != 0:
                    p.wait("sync", sv, p_done_v)
                    p.wait("sync", qsw[0], p_st)
                src = s_xn[s].rearrange("kc p t -> p kc t")
                if par == 0:
                    p.dma("sync", qxn[0], xnbuf3, src)
                else:
                    p.dma("sync", qxn[1], xnsA.rearrange("p (kc t) -> p kc t", t=SCT), src[:, 0:16])
                    p.dma("sync", qxn[1], xnsB.rearrange("p (kc t) -> p kc t", t=SCT), src[:, 16:24])
                    p.dma("sync", qxn[1], xnsC.rearrange("p (kc t) -> p kc t", t=SCT), src[:, 24:32])
                return qxn[par].n

            vload = {}
            vev = {}
            vmm_done = {}
            vload[0] = xn_load(0, 0, a1_done_t)
            first_vmm = True
            for s in range(NSC):
                par = s % 2
                if s + 1 < NSC:
                    vload[s + 1] = xn_load((s + 1) % 2, s + 1, vmm_done.get(s - 1))
                p.wait("tensor", qxn[s % 2], vload[s])
                p.wait("tensor", qw, wv_loaded)
                if first_vmm:
                    p.wait("tensor", sv, a1_done_v)
                    first_vmm = False
                for t in range(4):
                    blk = s * 4 + t
                    b0 = 0 if t % 2 == 0 else 2
                    for n0 in (0, 1):
                        g = vev.get((blk - 2, n0))
                        if g is not None:
                            p.wait("tensor", sv, g)
                    mm = None
                    for kc in range(NKC):
                        lhs = xn_view(par, kc)[:, t * 128:(t + 1) * 128]
                        for n0 in (0, 1):
                            bank = b0 + n0
                            mm = p.op("tensor", "matmul",
                                      psA[:, bank * 512:(bank + 1) * 512],
                                      lhs, W64[:, kc * KV + n0 * 512: kc * KV + (n0 + 1) * 512],
                                      start=(kc == 0), stop=(kc == NKC - 1), sem=te)
                    vmm_done[s] = mm
                    p.wait("vector", te, mm)
                    for n0 in (0, 1):
                        bank = b0 + n0
                        vev[(blk, n0)] = p.op(
                            "vector", "tensor_copy",
                            V_tm[:, blk * KV + n0 * 512: blk * KV + (n0 + 1) * 512],
                            psA[:, bank * 512:(bank + 1) * 512], sem=sv)
            a2_done_v = sv.n
            a2_done_t = te.n
            if "V" in tap:
                p.wait("sync", sv, a2_done_v)
                p.dma("sync", qt_, tap["V"], V_tm[:, :])

        accP = psT.bitcast(F32)
        lat2 = [lat, lbt]
        comb_done = {}
        lst_ld = {}
        tmp_store = [None, None]
        comb_k = [0]
        nlst = 2

        def emit_comb_steps(n_steps):
            for _ in range(n_steps):
                k = comb_k[0]
                if k >= 32:
                    return
                comb_k[0] += 1
                ph, n = divmod(k, 16)
                buf = lst_b[k % nlst]
                if k == 0:
                    p.wait("gpsimd", te, a2_done_t)
                    p.wait("gpsimd", sv, p_done_v)
                    p.wait("gpsimd", qsw[0], p_st)
                if k >= nlst:
                    p.wait("gpsimd", sv, comb_done[k - nlst])
                lst_ld[k] = p.dma("gpsimd", ql[k % 2], buf[:, :], lat2[ph][n])
                p.wait("vector", ql[k % 2], lst_ld[k])
                if n == 0:
                    if ph == 0:
                        p.wait("vector", qs, 32)
                        for bank in range(2):
                            if psT_guard[bank] is not None:
                                p.wait("vector", sv, psT_guard[bank])
                    comb_done[k] = p.op("vector", "tensor_scalar", accP[:, :],
                                        buf[:, :], gate_b[:, 0:1], None,
                                        ALU.mult, sem=sv)
                else:
                    comb_done[k] = p.op("vector", "scalar_tensor_tensor",
                                        accP[:, :], buf[:, :],
                                        gate_b[:, n:n + 1], accP[:, :],
                                        ALU.mult, ALU.add, sem=sv)
                p.wait("vector", sv, comb_done[k])
                if n == 15:
                    if ph == 1:
                        p.wait("vector", qts, tmp_store[0])
                    ev = p.op("vector", "tensor_copy", lst_b[2][:, :], accP[:, :],
                              sem=sv)
                    p.wait("scalar", sv, ev)
                    if ph == 0:
                        dst = s_tmpa.rearrange("(p a) r -> p (a r)", p=128)
                    else:
                        dst = s_tmpb.rearrange("r (a c) -> (r a) c", a=4)
                    tmp_store[ph] = p.dma("scalar", qts, dst, lst_b[2][:, :])
                    if ph == 1:
                        psT_guard[0] = psT_guard[1] = ev

        if want("A3"):
            p.wait("sync", te, a2_done_t)
            p.wait("sync", qsw[0], p_st)
            wqe3o = W64.rearrange("p (kc n) -> p kc n", n=KV)
            p.dma("sync", qw, wqe3o, s_wqe.rearrange("kc p n -> p kc n"))
            wqe_loaded = qw.n
            qev = {}
            first_q = True
            last_mm = None
            for s in range(2):
                if s > 0:
                    p.wait("sync", te, last_mm)
                p.dma("sync", qxn[0], xnbuf3, s_xn[s].rearrange("kc p t -> p kc t"))
                ld = qxn[0].n
                p.wait("tensor", qxn[0], ld)
                p.wait("tensor", qw, wqe_loaded)
                if first_q:
                    p.wait("tensor", sv, a2_done_v)
                    first_q = False
                for j in range(NH):
                    emit_comb_steps(2)
                    bank = j % 2
                    g = qev.get((s, j - 2)) if j >= 2 else qev.get((s - 1, j + 6))
                    if g is not None:
                        p.wait("tensor", ss, g)
                    mm = None
                    for kc in range(NKC):
                        mm = p.op("tensor", "matmul",
                                  psA[:, bank * 512:(bank + 1) * 512],
                                  W64[:, kc * KV + j * 128: kc * KV + (j + 1) * 128],
                                  xnbuf[:, kc * SCT:(kc + 1) * SCT],
                                  start=(kc == 0), stop=(kc == NKC - 1), sem=te)
                    last_mm = mm
                    p.wait("scalar", te, mm)
                    qev[(s, j)] = p.op("scalar", "activation",
                                       Q_fm[:, j * TOWN + s * SCT: j * TOWN + (s + 1) * SCT],
                                       psA[:, bank * 512:(bank + 1) * 512],
                                       AF.Copy, scale=INVSQ, sem=ss)
            emit_comb_steps(32)
            a3_done_t = te.n
            a3_done_s = ss.n
            if "Q" in tap:
                p.wait("sync", ss, a3_done_s)
                p.dma("sync", qt_, tap["Q"], Q_fm[:, :])

        if want("B"):
            p.wait("scalar", ql[0], lst_ld[30])
            p.wait("scalar", ql[1], lst_ld[31])
            p.wait("scalar", qts, tmp_store[1])
            p.wait("gpsimd", te, a3_done_t)
            wo3o = W64.rearrange("p (kc n) -> p kc n", n=D)
            wo3i = wo.rearrange("(kc p) n -> p kc n", p=128)
            for hv in range(2):
                p.dma("gpsimd", qw, wo3o[:, hv * 4:(hv + 1) * 4], wo3i[:, hv * 4:(hv + 1) * 4])
            wo_loaded = qw.n

            sexpB = stats[:, 8:9]
            nmx = stats[:, 9:10]
            sexp = stats[:, 10:11]
            rs = stats[:, 11:12]
            pmax = stats[:, 12:16]

            exp_done = {"h0": a3_done_s, "h1": a3_done_s}
            o_ev = {"last": None}
            wo_guard = [None, None]
            woev = {}
            xres_ld = {}
            ln2_done = {}

            def b_scores(qt, h):
                mms = []
                for n in range(4):
                    if n == 0:
                        p.wait("tensor", ss, exp_done["h0"])
                    elif n == 2:
                        p.wait("tensor", ss, exp_done["h1"])
                    mms.append(p.op("tensor", "matmul",
                                    psA[:, n * 512:(n + 1) * 512],
                                    Q_fm[:, h * TOWN + qt * 128: h * TOWN + (qt + 1) * 128],
                                    K_fm[:, h * TFULL + n * 512: h * TFULL + (n + 1) * 512],
                                    start=True, stop=True, sem=te))
                return mms

            def b_softmax(qt, h, sc_mms, pbuf):
                lastp = None
                for n in range(4):
                    p.wait("vector", te, sc_mms[n])
                    lastp = p.op("vector", "tensor_reduce", pmax[:, n:n + 1],
                                 psA[:, n * 512:(n + 1) * 512], AX.X, ALU.max, sem=sv)
                p.wait("vector", sv, lastp)
                v = p.op("vector", "tensor_reduce", nmx, pmax, AX.X, ALU.max,
                         negate=True, sem=sv)
                p.wait("scalar", sv, v)
                e0 = p.op("scalar", "activation", pbuf[:, 0:1024], psA[:, 0:1024],
                          AF.Exp, bias=nmx, sem=ss, accum_out=sexp)
                exp_done["h0"] = e0
                e1 = p.op("scalar", "activation", pbuf[:, 1024:2048], psA[:, 1024:2048],
                          AF.Exp, bias=nmx, sem=ss, accum_out=sexpB)
                exp_done["h1"] = e1
                p.wait("vector", ss, e1)
                sa = p.op("vector", "tensor_add", sexp, sexp, sexpB, sem=sv)
                p.wait("vector", sv, sa)
                rv = p.op("vector", "reciprocal", rs, sexp, sem=sv)
                p.wait("scalar", sv, rv)
                return p.op("scalar", "activation", pbuf[:, :], pbuf[:, :], AF.Identity,
                            scale=rs, sem=ss)

            def b_ptrans_and_o(h, ps_done, pbuf):
                mm = None
                for g in range(2):
                    bank = g % 2
                    p.wait("tensor", ss, ps_done)
                    if psT_guard[bank] is not None:
                        p.wait("tensor", sv, psT_guard[bank])
                    tv = None
                    for i in range(8):
                        blk = g * 8 + i
                        tv = p.op("tensor", "transpose",
                                  psT[:, bank * 1024 + i * 128: bank * 1024 + (i + 1) * 128],
                                  pbuf[:, blk * 128:(blk + 1) * 128], idn[:, :], sem=te)
                    p.wait("vector", te, tv)
                    dst = PT_sb.rearrange("p (b d) -> p b d", d=128)[:, g * 8:(g + 1) * 8]
                    psT_guard[bank] = p.op("vector", "tensor_copy", dst, psT4[:, bank], sem=sv)
                p.wait("tensor", sv, psT_guard[1])
                if o_ev["last"] is not None:
                    p.wait("tensor", ss, o_ev["last"])
                if wo_guard[1] is not None:
                    p.wait("tensor", sv, wo_guard[1])
                for blk in range(16):
                    mm = p.op("tensor", "matmul", psO[:, 0:128],
                              V_tm[:, blk * KV + h * 128: blk * KV + (h + 1) * 128],
                              PT_sb[:, blk * 128:(blk + 1) * 128],
                              start=(blk == 0), stop=(blk == 15), sem=te)
                p.wait("scalar", te, mm)
                o_ev["last"] = p.op("scalar", "activation", O_fm[:, h * 128:(h + 1) * 128],
                                    psO[:, 0:128], AF.Copy, sem=ss)

            xnew_st = {}
            ln2_trans = {}

            def emit_xres_load(qt):
                if qt >= 1:
                    p.wait("sync", sv, woev[qt - 1])
                    p.wait("sync", ss, ln2_done[qt - 1])
                    p.wait("sync", qt_, xnew_st[qt - 1])
                    p.wait("sync", te, ln2_trans[qt - 1])
                p.dma("sync", qs, xresA[:, :], x_own[qt * 128:(qt + 1) * 128, 0:2048])
                p.dma("sync", qs, xresB[:, :], x_own[qt * 128:(qt + 1) * 128, 2048:4096])
                xres_ld[qt] = qs.n

            def emit_wo(qt):
                p.wait("tensor", ss, o_ev["last"])
                p.wait("tensor", qw, wo_loaded)
                wops = [psW, psO]
                for cb in range(8):
                    bank = wops[cb % 2]
                    if wo_guard[cb % 2] is not None:
                        p.wait("tensor", sv, wo_guard[cb % 2])
                    mm = None
                    for kc in range(NH):
                        mm = p.op("tensor", "matmul", bank[:, 0:512],
                                  O_fm[:, kc * 128:(kc + 1) * 128],
                                  W64[:, kc * D + cb * 512: kc * D + (cb + 1) * 512],
                                  start=(kc == 0), stop=(kc == NH - 1), sem=te)
                    p.wait("vector", te, mm)
                    if cb == 0:
                        p.wait("vector", qs, xres_ld[qt])
                    xr = xresA if cb < 4 else xresB
                    co = (cb % 4) * 512
                    wo_guard[cb % 2] = p.op("vector", "tensor_add",
                                            xr[:, co:co + 512], xr[:, co:co + 512],
                                            bank[:, 0:512], sem=sv)
                woev[qt] = wo_guard[1]
                p.wait("sync", sv, woev[qt])
                p.dma("sync", qt_, s_xnew[qt * 128:(qt + 1) * 128, 0:2048], xresA[:, :])
                xnew_st[qt] = p.dma("sync", qt_, s_xnew[qt * 128:(qt + 1) * 128, 2048:4096],
                                    xresB[:, :])

            def emit_ln2_stats(qt):
                p.wait("scalar", sv, woev[qt])
                p.wait("scalar", qt_, xnew_st[qt])
                if qt >= 1:
                    p.wait("scalar", te, ln2_trans[qt - 1])
                ln2_done[qt] = ln_chunk([(xresA[:, :], xn2dA[:, :]),
                                         (xresB[:, :], xn2dB[:, :])])

            def emit_ln2_trans(qt):
                s2, c2 = divmod(qt, 4)
                dumps = [xn2dA, xn2dB]
                for gg in range(4):
                    bank = gg % 2
                    p.wait("tensor", ss, ln2_done[qt])
                    if c2 == 0 and s2 >= 1 and gg == 0:
                        p.wait("vector", qt_, xn2_store[s2 - 1])
                    if psT_guard[bank] is not None:
                        p.wait("tensor", sv, psT_guard[bank])
                    tv = None
                    for i in range(8):
                        kc = gg * 8 + i
                        src = dumps[kc // 16]
                        tv = p.op("tensor", "transpose",
                                  psT[:, bank * 1024 + i * 128: bank * 1024 + (i + 1) * 128],
                                  src[:, (kc % 16) * 128:(kc % 16 + 1) * 128],
                                  idn[:, :], sem=te)
                    p.wait("vector", te, tv)
                    dst = xnbuf3[:, gg * 8:(gg + 1) * 8, c2 * 128:(c2 + 1) * 128]
                    psT_guard[bank] = p.op("vector", "tensor_copy", dst, psT4[:, bank], sem=sv)
                ln2_trans[qt] = te.n
                if c2 == 3:
                    p.wait("sync", sv)
                    xn2_store[s2] = p.dma("sync", qt_,
                                          s_xn2[s2].rearrange("kc p t -> p kc t"), xnbuf3)

            emit_xres_load(0)
            pend = None
            for qt in range(NQT):
                for h in range(NH):
                    sc = b_scores(qt, h)
                    if qt >= 1 and h == 1:
                        emit_wo(qt - 1)
                    if pend is not None:
                        b_ptrans_and_o(*pend)
                    pend_ps = b_softmax(qt, h, sc, P_tm[h % 2])
                    if qt == 0 and h == 0 and "P0" in tap:
                        p.wait("sync", ss, pend_ps)
                        p.dma("sync", qt_, tap["P0"], P_tm[0][:, :])
                        p.wait("scalar", qt_)
                    pend = (h, pend_ps, P_tm[h % 2])
                    if qt >= 1:
                        if h == 2:
                            emit_ln2_stats(qt - 1)
                        elif h == 5:
                            emit_ln2_trans(qt - 1)
                        elif h == 6:
                            emit_xres_load(qt)
            b_ptrans_and_o(*pend)
            emit_wo(NQT - 1)
            emit_ln2_stats(NQT - 1)
            emit_ln2_trans(NQT - 1)
            b_done_t = te.n
            b_done_v = sv.n
            b_done_s = ss.n
            if "XNEW" in tap:
                p.wait("sync", qt_)
                p.dma("sync", qt_, tap["XNEW"], s_xnew)
            if "XN2" in tap:
                p.wait("sync", qt_)
                for s2 in range(2):
                    p.dma("sync", qt_, tap["XN2"][s2], s_xn2[s2])

        if want("C"):
            xnew_stores = qt_.n

            k3o = ker_bf.rearrange("p (kc n) -> p kc n", n=2048)
            q_loaded = {}
            ker_free = {}

            def emit_ker_quarter(hf, q):
                if hf == 0 and q == 0:
                    p.wait("gpsimd", te, b_done_t)
                    p.wait("gpsimd", ss, b_done_s)
                    p.wait("gpsimd", sv, b_done_v)
                    p.wait("gpsimd", qt_, xnew_stores)
                if hf == 1:
                    p.wait("gpsimd", te, ker_free[q])
                k3i = ker.rearrange("(kc p) n -> p kc n", p=128)[:, :, hf * 2048:(hf + 1) * 2048]
                q_loaded[(hf, q)] = p.dma("gpsimd", qk[q], k3o[:, q * 8:(q + 1) * 8],
                                          k3i[:, q * 8:(q + 1) * 8])

            for q in range(4):
                emit_ker_quarter(0, q)
            p.wait("gpsimd", qts, tmp_store[1])
            acat3 = A_catT.rearrange("p (kc r) -> p kc r", r=R)
            p.dma("gpsimd", ql[0], acat3[:, :, 0:32], s_tmpa.rearrange("(kc p) r -> p kc r", p=128))
            p.dma("gpsimd", ql[0], acat3[:, :, 32:64], last.rearrange("(kc p) r -> p kc r", p=128))
            p.dma("gpsimd", ql[0], B_catT[0:32, :], s_tmpb)
            p.dma("gpsimd", ql[0], B_catT[32:64, :], lbst)
            lora_ready = ql[0].n

            woev2 = {}
            kermm_done = None
            ax_ready = {}
            loaded_s2 = [None]
            idx = 0
            for hf in range(2):
                if hf == 0:
                    p.wait("sync", te, b_done_t)
                    p.wait("sync", ss, b_done_s)
                    p.wait("sync", sv, b_done_v)
                    p.wait("sync", qt_, xnew_stores)
                p.dma("sync", qsw[hf], bias_h[:, :], bias_b[:, hf * 2048:(hf + 1) * 2048])
                bias_ld = qsw[hf].n
                for s2 in ((0, 1) if hf == 0 else (1, 0)):
                    if loaded_s2[0] != s2:
                        if (hf, s2) != (0, 0):
                            p.wait("sync", te, kermm_done)
                        if hf == 0 and want("B"):
                            p.wait("sync", qt_, xn2_store[s2])
                        p.dma("sync", qx2, xnbuf3, s_xn2[s2].rearrange("kc p t -> p kc t"))
                        xn2_ld = qx2.n
                        loaded_s2[0] = s2
                    for t in range(4):
                        chunk = s2 * 4 + t
                        g = woev2.get(idx - 2)
                        if g is not None:
                            p.wait("sync", sv, g)
                        p.dma("sync", qb[idx % 2], xnb[idx % 2][:, :],
                              s_xnew[chunk * 128:(chunk + 1) * 128, hf * 2048:(hf + 1) * 2048])
                        xnew_ld = qb[idx % 2].n
                        p.wait("tensor", qx2, xn2_ld)
                        g = woev2.get(idx - 1)
                        if g is not None:
                            p.wait("tensor", sv, g)
                        if idx == 0:
                            p.wait("tensor", ss, b_done_s)
                        mm = None
                        for kc in range(NKC):
                            if kc % 8 == 0:
                                p.wait("tensor", qk[kc // 8], q_loaded[(hf, kc // 8)])
                            lhs = xnbuf[:, kc * SCT + t * 128: kc * SCT + (t + 1) * 128]
                            for cb in range(4):
                                mm = p.op("tensor", "matmul", psA[:, cb * 512:(cb + 1) * 512],
                                          lhs,
                                          ker_bf[:, kc * 2048 + cb * 512: kc * 2048 + (cb + 1) * 512],
                                          start=(kc == 0), stop=False, sem=te)
                            if hf == 0 and idx == 7 and kc % 8 == 7:
                                ker_free[kc // 8] = te.n
                                emit_ker_quarter(1, kc // 8)
                        if hf == 0 and t == 0:
                            p.wait("tensor", ql[0], lora_ready)
                            if s2 == 0:
                                p.wait("tensor", sv, b_done_v)
                            else:
                                p.wait("tensor", ss, ax_ready[1 - s2])
                            axm = None
                            for kc in range(NKC):
                                axm = p.op("tensor", "matmul", psW[0:64, 0:512],
                                           A_catT[:, kc * R:(kc + 1) * R],
                                           xnbuf[:, kc * SCT:(kc + 1) * SCT],
                                           start=(kc == 0), stop=(kc == NKC - 1), sem=te)
                            p.wait("scalar", te, axm)
                            if s2 == 0:
                                p.wait("scalar", qt_, xnew_stores)
                            ax_ready[s2] = p.op("scalar", "activation",
                                                Ax_bf[:, s2 * SCT:(s2 + 1) * SCT],
                                                psW[0:64, 0:512], AF.Copy, sem=ss)
                        p.wait("tensor", ss, ax_ready[s2])
                        for cb in range(4):
                            mm = p.op("tensor", "matmul", psA[:, cb * 512:(cb + 1) * 512],
                                      Ax_bf[:, chunk * 128:(chunk + 1) * 128],
                                      B_catT[:, hf * 2048 + cb * 512: hf * 2048 + (cb + 1) * 512],
                                      start=False, stop=True, sem=te)
                        kermm_done = mm
                        p.wait("vector", te, kermm_done)
                        p.wait("vector", qb[idx % 2], xnew_ld)
                        p.wait("vector", qsw[hf], bias_ld)
                        for cb in range(4):
                            p.op("vector", "tensor_add", xnb[idx % 2][:, cb * 512:(cb + 1) * 512],
                                 xnb[idx % 2][:, cb * 512:(cb + 1) * 512],
                                 psA[:, cb * 512:(cb + 1) * 512])
                        woev2[idx] = p.op("vector", "tensor_add", xnb[idx % 2][:, :],
                                          xnb[idx % 2][:, :], bias_h[:, :], sem=sv)
                        p.wait("sync", sv, woev2[idx])
                        p.dma("sync", qt_, out[chunk * 128:(chunk + 1) * 128,
                                              hf * 2048:(hf + 1) * 2048],
                              xnb[idx % 2][:, :])
                        idx += 1

        p.wait("sync", qt_)

        with nc.Block() as block:
            @block.sync
            def _(e):
                for f in p.q["sync"]:
                    f(e)

            @block.scalar
            def _(e):
                for f in p.q["scalar"]:
                    f(e)

            @block.vector
            def _(e):
                for f in p.q["vector"]:
                    f(e)

            @block.tensor
            def _(e):
                for f in p.q["tensor"]:
                    f(e)

            @block.gpsimd
            def _(e):
                for f in p.q["gpsimd"]:
                    f(e)

    return nc


def make_in_maps(inputs):
    X = np.asarray(inputs["L0_attn_x"], dtype=np.float32)
    wq = np.ascontiguousarray(np.asarray(inputs["L0_attn_wq"], np.float32))
    wk = np.ascontiguousarray(np.asarray(inputs["L0_attn_wk"], np.float32))
    wv = np.ascontiguousarray(np.asarray(inputs["L0_attn_wv"], np.float32))
    wo = np.ascontiguousarray(np.asarray(inputs["L0_attn_wo"], np.float32))
    gate = np.asarray(inputs["L0_ffn_gate"], np.float32)
    ker = np.ascontiguousarray(np.asarray(inputs["L0_ffn_kernel"], np.float32))
    bias = np.asarray(inputs["L0_ffn_bias"], np.float32)
    lA = np.asarray(inputs["L0_ffn_lora_A"], np.float32)
    lB = np.asarray(inputs["L0_ffn_lora_B"], np.float32)
    lAs = np.asarray(inputs["L0_ffn_lora_A_shared"], np.float32)
    lBs = np.asarray(inputs["L0_ffn_lora_B_shared"], np.float32)

    lat = np.ascontiguousarray(lA.transpose(0, 2, 1)).reshape(16, 128, 1024)
    lbt = np.ascontiguousarray(lB.transpose(0, 2, 1)).reshape(16, 128, 1024)
    last = np.ascontiguousarray(lAs.T)
    lbst = np.ascontiguousarray(lBs.T)
    bias_b = np.ascontiguousarray(np.broadcast_to(bias, (128, 4096)))

    in_maps = []
    for c in range(8):
        b, h = divmod(c, 2)
        gate_b = np.ascontiguousarray(np.broadcast_to(gate[2 * b + h], (128, 16)))
        in_maps.append({
            "x_own": np.ascontiguousarray(X[b, h * 1024:(h + 1) * 1024]),
            "x_oth": np.ascontiguousarray(X[b, (1 - h) * 1024:(2 - h) * 1024]),
            "wq": wq, "wk": wk, "wv": wv, "wo": wo, "ker": ker,
            "bias_b": bias_b, "gate_b": gate_b,
            "lat": lat, "lbt": lbt, "last": last, "lbst": lbst,
        })
    return in_maps


def kernel(**inputs):
    nc = build_nc()
    in_maps = make_in_maps(inputs)
    res = run_bass_kernel_spmd(nc, in_maps, core_ids=list(range(8)))
    out = np.empty((4, 2048, 4096), np.float32)
    for c, r in enumerate(res.results):
        b, h = divmod(c, 2)
        out[b, h * 1024:(h + 1) * 1024] = r["out"]
    return out


# version 19 (final)
# speedup vs baseline: 1.0400x; 1.0343x over previous
import contextlib
import numpy as np
import ml_dtypes
import concourse.bass as bass
import concourse.mybir as mybir
from concourse.bass_utils import run_bass_kernel_spmd

F32 = mybir.dt.float32
BF16 = mybir.dt.bfloat16
AF = mybir.ActivationFunctionType
ALU = mybir.AluOpType
AX = mybir.AxisListType

D = 4096
KV = 1024
NH = 8
NKC = 32
TOWN = 1024
TFULL = 2048
NSC = 4
SCT = 512
NQT = 8
R = 64
INVSQ = float(1.0 / np.sqrt(128.0))
EPS = 1e-5

SBUF_BASE = 16512
KiB = 1024

ENGINES = ("sync", "scalar", "vector", "tensor", "gpsimd")


class Sem:
    __slots__ = ("h", "n")

    def __init__(self, h):
        self.h = h
        self.n = 0


class P:

    def __init__(self, nc):
        self.nc = nc
        self.q = {e: [] for e in ENGINES}

    def emit(self, eng, fn):
        self.q[eng].append(fn)

    def dma(self, eng, sem, out, in_):
        sem.n += 16
        self.emit(eng, lambda e, o=out, i=in_, h=sem.h: e.dma_start(out=o, in_=i).then_inc(h, 16))
        return sem.n

    def wait(self, eng, sem, val=None):
        v = sem.n if val is None else val
        if v is None or v <= 0:
            return
        self.emit(eng, lambda e, h=sem.h, v=v: e.wait_ge(h, v))

    def op(self, eng, name, *args, sem=None, **kw):
        if sem is not None:
            sem.n += 1
        s = sem

        def f(e, name=name, args=args, kw=kw, s=s):
            ins = getattr(e, name)(*args, **kw)
            if s is not None:
                ins.then_inc(s.h, 1)
        self.emit(eng, f)
        return None if s is None else s.n


def build_nc(taps=(), upto="C"):
    nc = bass.Bass("TRN2", target_bir_lowering=False)
    phase_order = ["P", "A1", "A2", "A3", "B", "C"]
    upto_i = phase_order.index(upto)

    def want(ph):
        return phase_order.index(ph) <= upto_i

    x_own = nc.dram_tensor("x_own", [TOWN, D], F32, kind="ExternalInput").ap()
    x_oth = nc.dram_tensor("x_oth", [TOWN, D], F32, kind="ExternalInput").ap()
    wq = nc.dram_tensor("wq", [D, D], F32, kind="ExternalInput").ap()
    wk = nc.dram_tensor("wk", [D, KV], F32, kind="ExternalInput").ap()
    wv = nc.dram_tensor("wv", [D, KV], F32, kind="ExternalInput").ap()
    wo = nc.dram_tensor("wo", [KV, D], F32, kind="ExternalInput").ap()
    ker = nc.dram_tensor("ker", [D, D], F32, kind="ExternalInput").ap()
    bias_b = nc.dram_tensor("bias_b", [128, D], F32, kind="ExternalInput").ap()
    gate_bi = nc.dram_tensor("gate_b", [128, 16], F32, kind="ExternalInput").ap()
    lat = nc.dram_tensor("lat", [16, 128, 1024], F32, kind="ExternalInput").ap()
    lbt = nc.dram_tensor("lbt", [16, 128, 1024], F32, kind="ExternalInput").ap()
    last = nc.dram_tensor("last", [D, 32], F32, kind="ExternalInput").ap()
    lbst = nc.dram_tensor("lbst", [32, D], F32, kind="ExternalInput").ap()

    out = nc.dram_tensor("out", [TOWN, D], F32, kind="ExternalOutput").ap()

    s_wqe = nc.dram_tensor("s_wqe", [NKC, 128, KV], BF16, kind="Internal").ap()
    s_xn = nc.dram_tensor("s_xn", [NSC, NKC, 128, SCT], BF16, kind="Internal").ap()
    s_xn2 = nc.dram_tensor("s_xn2", [2, NKC, 128, SCT], BF16, kind="Internal").ap()
    s_xnew = nc.dram_tensor("s_xnew", [TOWN, D], F32, kind="Internal").ap()
    s_tmpa = nc.dram_tensor("s_tmpa", [D, 32], BF16, kind="Internal").ap()
    s_tmpb = nc.dram_tensor("s_tmpb", [32, D], BF16, kind="Internal").ap()

    ident = nc.inline_tensor(np.eye(128, dtype=ml_dtypes.bfloat16), name="ident").ap()

    tap = {}
    for name, shape, dt in [
        ("K", [128, NH * TFULL], BF16), ("V", [128, 16 * KV], BF16),
        ("Q", [128, NH * TOWN], BF16), ("XN", [NSC, NKC, 128, SCT], BF16),
        ("XNEW", [TOWN, D], F32), ("XN2", [2, NKC, 128, SCT], BF16),
        ("WQE", [NKC, 128, KV], BF16),
        ("P0", [128, TFULL], BF16), ("O0", [128, NH * 128], BF16),
        ("XR0", [128, D], F32), ("PT0", [128, 16 * 128], BF16),
    ]:
        if name in taps:
            tap[name] = nc.dram_tensor("tap_" + name, shape, dt, kind="ExternalOutput").ap()

    def sb(name, shape, dt, off_kib):
        return nc.alloc_sbuf_tensor_at(name, shape, dt, offset=SBUF_BASE + int(off_kib * KiB)).ap()

    xnbuf = sb("xnbuf", [128, NKC * SCT], BF16, 0)
    V_tm = sb("V_tm", [128, 16 * KV], BF16, 32)
    xbuf = [sb("xbuf0", [128, D], F32, 32),
            sb("xbuf1", [128, D], F32, 48)]
    xn_tm = sb("xn_tm", [128, D], BF16, 64)
    xresA = sb("xresA", [128, 2048], F32, 64)
    Q_fm = sb("Q_fm", [128, NH * TOWN], BF16, 72)
    xnsA = sb("xnsA", [128, 16 * SCT], BF16, 72)
    xnsB = sb("xnsB", [128, 8 * SCT], BF16, 88)
    P_tm = [sb("P_tm0", [128, TFULL], BF16, 88),
            sb("P_tm1", [128, TFULL], BF16, 92)]
    xn2dA = sb("xn2dA", [128, 2048], BF16, 64)
    xn2dB = sb("xn2dB", [128, 2048], BF16, 192)
    W64 = sb("W64", [128, NKC * KV], BF16, 96)
    K_fm = sb("K_fm", [128, NH * TFULL], BF16, 160)
    xnsC = sb("xnsC", [128, 8 * SCT], BF16, 192)
    xresB = sb("xresB", [128, 2048], F32, 192)
    wqb = [sb("wqb0", [128, D], BF16, 72), sb("wqb1", [128, D], BF16, 80)]
    ptmpb = sb("ptmpb", [128, KV], BF16, 88)
    wqecP = sb("wqecP", [128, 2 * KV], BF16, 90)
    wqec = [wqecP[:, 0:KV], wqecP[:, KV:2 * KV]]
    ker_bf = sb("ker_bf", [128, NKC * 2048], BF16, 32)
    xnb = [sb("xnb0", [128, 2048], F32, 160), sb("xnb1", [128, 2048], F32, 168)]
    bias_h = sb("bias_h", [128, 2048], F32, 176)
    A_catT = sb("A_catT", [128, NKC * R], BF16, 184)
    B_catT = sb("B_catT", [R, D], BF16, 188)
    Ax_bf = sb("Ax_bf", [R, TOWN], BF16, 196)
    idn = sb("idn", [128, 128], BF16, 200)
    gate_b = sb("gate_sb", [128, 16], F32, 200.25)
    stats = sb("stats", [128, 64], F32, 200.5)
    ceps = sb("ceps", [128, 1], F32, 200.40625)
    O_fm = sb("O_fm", [128, NH * 128], BF16, 201)
    PT_sb = sb("PT_sb", [128, 16 * 128], BF16, 203)
    lst_b = [sb(f"lstb{i}", [128, KV], BF16, 88 + 2 * i) for i in range(3)]

    psA = nc.alloc_psum_tensor("psA", [128, 4 * 512], F32).ap()
    psT = nc.alloc_psum_tensor("psT", [128, 2048], BF16).ap()
    psO = nc.alloc_psum_tensor("psO", [128, 512], F32).ap()
    psW = nc.alloc_psum_tensor("psW", [128, 512], F32).ap()

    p = P(nc)
    xnbuf3 = xnbuf.rearrange("p (kc t) -> p kc t", t=SCT)
    psT4 = psT.rearrange("p (b i d) -> p b i d", b=2, d=128)

    with contextlib.ExitStack() as stx:
        mk = lambda nm: Sem(stx.enter_context(nc.semaphore(nm)))
        qs = mk("qs")
        qt_ = mk("qt")
        qg = mk("qg")
        sv = mk("sv")
        ss = mk("ss")
        te = mk("te")
        qw = mk("qw")
        qp = [mk("qp0"), mk("qp1")]
        qsw = [mk("qsw0"), mk("qsw1")]
        qxn = [mk("qxn0"), mk("qxn1")]
        qk = [mk(f"qk{i}") for i in range(4)]
        ql = [mk("ql0"), mk("ql1")]
        qts = mk("qts")
        qb = [mk("qb0"), mk("qb1")]
        qx2 = mk("qx2")
        qx = [mk("qx0"), mk("qx1")]
        qxs = mk("qxs")
        qo = mk("qo")

        psT_guard = [None, None]

        p.dma("sync", qs, idn[:, :], ident)
        p.dma("sync", qs, gate_b[:, :], gate_bi)
        eps_set = p.op("vector", "memset", ceps[:, :], EPS, sem=sv)
        p.wait("scalar", sv, eps_set)

        a1_done_v = a1_done_t = a2_done_v = a2_done_t = None
        a3_done_t = a3_done_s = b_done_t = b_done_v = b_done_s = None
        xn_store = {}
        xn2_store = {}

        presum_done = {}
        wqe_load = {}
        wqe_stored = {}

        def emit_presum_load(kc):
            buf = wqb[kc % 2]
            if kc >= 2:
                p.wait("gpsimd", sv, presum_done[kc - 2])
            wqe_load[kc] = p.dma("gpsimd", qp[kc % 2], buf[:, :], wq[kc * 128:(kc + 1) * 128, :])

        def emit_presum_compute(kc):
            buf = wqb[kc % 2]
            p.wait("vector", qp[kc % 2], wqe_load[kc])
            if kc >= 1:
                p.wait("vector", sv, presum_done[kc - 1])
            v4 = buf.rearrange("p (h g d) -> p g h d", g=4, d=128)
            pt3 = ptmpb.rearrange("p (h d) -> p h d", d=128)
            t1 = p.op("vector", "tensor_add", pt3, v4[:, 0], v4[:, 1], sem=sv)
            p.wait("vector", sv, t1)
            t2 = p.op("vector", "tensor_add", pt3, pt3, v4[:, 2], sem=sv)
            p.wait("vector", sv, t2)
            wc = wqec[kc % 2]
            if kc >= 2:
                g = wqe_stored.get(kc - 1 if kc % 2 == 0 else kc - 2)
                if g is not None:
                    p.wait("vector", qsw[0], g)
            presum_done[kc] = p.op("vector", "tensor_add",
                                   wc.rearrange("p (h d) -> p h d", d=128),
                                   pt3, v4[:, 3], sem=sv)
            if kc % 2 == 1:
                p.wait("sync", sv, presum_done[kc])
                wqe_stored[kc] = p.dma(
                    "sync", qsw[0],
                    s_wqe[kc - 1:kc + 1].rearrange("two p n -> p two n"),
                    wqecP.rearrange("p (two n) -> p two n", two=2))

        def ln_chunk(pieces):
            mean, var, tvar, tsd, rstd, negmr = (stats[:, i:i + 1] for i in range(6))
            mv = stats[:, 0:2]
            k = 0
            bl = None
            for src, dump in pieces:
                w = (D // len(pieces)) // 512
                for t0 in range(w):
                    bl = p.op("vector", "bn_stats", stats[:, 16 + 6 * k:22 + 6 * k],
                              src[:, t0 * 512:(t0 + 1) * 512], sem=sv)
                    k += 1
            p.wait("vector", sv, bl)
            ag = p.op("vector", "bn_aggr", mv, stats[:, 16:16 + 6 * k], sem=sv)
            p.wait("vector", sv, ag)
            v2 = p.op("vector", "tensor_scalar", tvar, var, 1.0, EPS,
                      ALU.mult, ALU.add, sem=sv)
            p.wait("scalar", sv, v2)
            s3 = p.op("scalar", "activation", tsd, tvar, AF.Sqrt, sem=ss)
            p.wait("vector", ss, s3)
            rv = p.op("vector", "reciprocal", rstd, tsd, sem=sv)
            p.wait("vector", sv, rv)
            v3 = p.op("vector", "tensor_scalar", negmr, mean, rstd, -1.0,
                      ALU.mult, ALU.mult, sem=sv)
            p.wait("scalar", sv, v3)
            r = None
            for src, dump in pieces:
                r = p.op("scalar", "activation", dump, src, AF.Identity,
                         scale=rstd, bias=negmr, sem=ss)
            return r

        def transpose_chunk(dump_bf, dst3, c, norm_done):
            tv = ev = None
            for g in range(4):
                bank = g % 2
                p.wait("tensor", ss, norm_done)
                if psT_guard[bank] is not None:
                    p.wait("tensor", sv, psT_guard[bank])
                for i in range(8):
                    kc = g * 8 + i
                    tv = p.op("tensor", "transpose",
                              psT[:, bank * 1024 + i * 128: bank * 1024 + (i + 1) * 128],
                              dump_bf[:, kc * 128:(kc + 1) * 128], idn[:, :], sem=te)
                p.wait("vector", te, tv)
                dst = dst3[:, g * 8:(g + 1) * 8, c * 128:(c + 1) * 128]
                ev = p.op("vector", "tensor_copy", dst, psT4[:, bank], sem=sv)
                psT_guard[bank] = ev
            return tv, ev

        if want("A1"):
            p.wait("tensor", qs, 32)
            emit_presum_load(0)
            emit_presum_load(1)
            wk3o = W64.rearrange("p (kc n) -> p kc n", n=KV)
            wk3i = wk.rearrange("(kc p) n -> p kc n", p=128)
            for qtr in range(4):
                p.dma("gpsimd", qw, wk3o[:, qtr * 8:(qtr + 1) * 8], wk3i[:, qtr * 8:(qtr + 1) * 8])
            wk_loaded = qw.n

            kev = {}
            kmm_done = {}
            ln_done = {}
            trans_done = {}
            for i in range(16):
                s, c = divmod(i, 4)
                src = x_own if i < 8 else x_oth
                row = (i % 8) * 128
                if i >= 2:
                    p.wait("scalar", ss, ln_done[i - 2])
                xld = p.dma("scalar", qx[i % 2], xbuf[i % 2][:, :], src[row:row + 128, :])
                p.wait("scalar", qx[i % 2], xld)
                p.wait("vector", qx[i % 2], xld)
                if i >= 1:
                    p.wait("scalar", te, trans_done[i - 1])
                ln_done[i] = ln_chunk([(xbuf[i % 2][:, :], xn_tm[:, :])])
                if i > 0:
                    emit_presum_compute(2 * (i - 1))
                    emit_presum_compute(2 * (i - 1) + 1)
                    emit_presum_load(2 * i)
                    emit_presum_load(2 * i + 1)
                if c == 0 and s >= 1:
                    p.wait("vector", te, kmm_done[s - 1])
                    p.wait("vector", qt_, xn_store[s - 1])
                trans_done[i], _ = transpose_chunk(xn_tm[:, :], xnbuf3, c, ln_done[i])
                if c == 3:
                    p.wait("sync", sv)
                    xn_store[s] = p.dma("sync", qt_, s_xn[s].rearrange("kc p t -> p kc t"),
                                        xnbuf3)
                    p.wait("tensor", sv)
                    p.wait("tensor", qw, wk_loaded)
                    for j in range(NH):
                        bank = j % 2
                        if j >= 2 and kev.get(j - 2) is not None:
                            p.wait("tensor", sv, kev[j - 2])
                        mm = None
                        for kc in range(NKC):
                            mm = p.op("tensor", "matmul",
                                      psA[:, bank * 512:(bank + 1) * 512],
                                      W64[:, kc * KV + j * 128: kc * KV + (j + 1) * 128],
                                      xnbuf[:, kc * SCT:(kc + 1) * SCT],
                                      start=(kc == 0), stop=(kc == NKC - 1), sem=te)
                        kmm_done[s] = mm
                        p.wait("vector", te, mm)
                        kev[j] = p.op("vector", "tensor_copy",
                                      K_fm[:, j * TFULL + s * SCT: j * TFULL + (s + 1) * SCT],
                                      psA[:, bank * 512:(bank + 1) * 512], sem=sv)
            emit_presum_compute(NKC - 2)
            emit_presum_compute(NKC - 1)
            a1_done_v = sv.n
            a1_done_t = te.n
            p_done_v = presum_done[NKC - 1]
            p_st = wqe_stored[NKC - 1]
            if "WQE" in tap:
                p.wait("sync", qsw[0], p_st)
                for kc in range(NKC):
                    p.dma("sync", qt_, tap["WQE"][kc], s_wqe[kc])
            if "XN" in tap:
                p.wait("sync", qt_)
                for s in range(NSC):
                    p.dma("sync", qt_, tap["XN"][s], s_xn[s])
            if "K" in tap:
                p.wait("sync", sv, a1_done_v)
                p.dma("sync", qt_, tap["K"], K_fm[:, :])

        if want("A2"):
            p.wait("gpsimd", te, a1_done_t)
            wv3o = W64.rearrange("p (kc n) -> p kc n", n=KV)
            wv3i = wv.rearrange("(kc p) n -> p kc n", p=128)
            for qtr in range(4):
                p.dma("gpsimd", qw, wv3o[:, qtr * 8:(qtr + 1) * 8], wv3i[:, qtr * 8:(qtr + 1) * 8])
            wv_loaded = qw.n

            def xn_view(par, kc):
                if par == 0:
                    return xnbuf[:, kc * SCT:(kc + 1) * SCT]
                if kc < 16:
                    return xnsA[:, kc * SCT:(kc + 1) * SCT]
                if kc < 24:
                    return xnsB[:, (kc - 16) * SCT:(kc - 15) * SCT]
                return xnsC[:, (kc - 24) * SCT:(kc - 23) * SCT]

            def xn_load(par, s, guard_te):
                p.wait("sync", qt_, xn_store[NSC - 1] if par == 0 else xn_store[s])
                if guard_te is not None:
                    p.wait("sync", te, guard_te)
                if par != 0:
                    p.wait("sync", sv, p_done_v)
                    p.wait("sync", qsw[0], p_st)
                src = s_xn[s].rearrange("kc p t -> p kc t")
                if par == 0:
                    p.dma("sync", qxn[0], xnbuf3, src)
                else:
                    p.dma("sync", qxn[1], xnsA.rearrange("p (kc t) -> p kc t", t=SCT), src[:, 0:16])
                    p.dma("sync", qxn[1], xnsB.rearrange("p (kc t) -> p kc t", t=SCT), src[:, 16:24])
                    p.dma("sync", qxn[1], xnsC.rearrange("p (kc t) -> p kc t", t=SCT), src[:, 24:32])
                return qxn[par].n

            vload = {}
            vev = {}
            vmm_done = {}
            vload[0] = xn_load(0, 0, a1_done_t)
            first_vmm = True
            for s in range(NSC):
                par = s % 2
                if s + 1 < NSC:
                    vload[s + 1] = xn_load((s + 1) % 2, s + 1, vmm_done.get(s - 1))
                p.wait("tensor", qxn[s % 2], vload[s])
                p.wait("tensor", qw, wv_loaded)
                if first_vmm:
                    p.wait("tensor", sv, a1_done_v)
                    first_vmm = False
                for t in range(4):
                    blk = s * 4 + t
                    b0 = 0 if t % 2 == 0 else 2
                    for n0 in (0, 1):
                        g = vev.get((blk - 2, n0))
                        if g is not None:
                            p.wait("tensor", sv, g)
                    mm = None
                    for kc in range(NKC):
                        lhs = xn_view(par, kc)[:, t * 128:(t + 1) * 128]
                        for n0 in (0, 1):
                            bank = b0 + n0
                            mm = p.op("tensor", "matmul",
                                      psA[:, bank * 512:(bank + 1) * 512],
                                      lhs, W64[:, kc * KV + n0 * 512: kc * KV + (n0 + 1) * 512],
                                      start=(kc == 0), stop=(kc == NKC - 1), sem=te)
                    vmm_done[s] = mm
                    p.wait("vector", te, mm)
                    for n0 in (0, 1):
                        bank = b0 + n0
                        vev[(blk, n0)] = p.op(
                            "vector", "tensor_copy",
                            V_tm[:, blk * KV + n0 * 512: blk * KV + (n0 + 1) * 512],
                            psA[:, bank * 512:(bank + 1) * 512], sem=sv)
            a2_done_v = sv.n
            a2_done_t = te.n
            if "V" in tap:
                p.wait("sync", sv, a2_done_v)
                p.dma("sync", qt_, tap["V"], V_tm[:, :])

        accP = psT.bitcast(F32)
        lat2 = [lat, lbt]
        comb_done = {}
        lst_ld = {}
        tmp_store = [None, None]
        acc_evac = [None, None]
        comb_k = [0]
        nlst = 2

        def emit_comb_steps(n_steps):
            for _ in range(n_steps):
                k = comb_k[0]
                if k >= 32:
                    return
                comb_k[0] += 1
                ph, n = divmod(k, 16)
                buf = lst_b[k % nlst]
                if k == 0:
                    p.wait("gpsimd", te, a2_done_t)
                    p.wait("gpsimd", sv, p_done_v)
                    p.wait("gpsimd", qsw[0], p_st)
                if k >= nlst:
                    p.wait("gpsimd", sv, comb_done[k - nlst])
                lst_ld[k] = p.dma("gpsimd", ql[k % 2], buf[:, :], lat2[ph][n])
                p.wait("vector", ql[k % 2], lst_ld[k])
                if n == 0:
                    if ph == 0:
                        p.wait("vector", qs, 32)
                        for bank in range(2):
                            if psT_guard[bank] is not None:
                                p.wait("vector", sv, psT_guard[bank])
                    else:
                        p.wait("vector", sv, acc_evac[0])
                    comb_done[k] = p.op("vector", "tensor_scalar", accP[:, :],
                                        buf[:, :], gate_b[:, 0:1], None,
                                        ALU.mult, sem=sv)
                else:
                    comb_done[k] = p.op("vector", "scalar_tensor_tensor",
                                        accP[:, :], buf[:, :],
                                        gate_b[:, n:n + 1], accP[:, :],
                                        ALU.mult, ALU.add, sem=sv)
                p.wait("vector", sv, comb_done[k])
                if n == 15:
                    if ph == 1:
                        p.wait("vector", qts, tmp_store[0])
                    ev = p.op("vector", "tensor_copy", lst_b[2][:, :], accP[:, :],
                              sem=sv)
                    acc_evac[ph] = ev
                    p.wait("scalar", sv, ev)
                    if ph == 0:
                        dst = s_tmpa.rearrange("(p a) r -> p (a r)", p=128)
                    else:
                        dst = s_tmpb.rearrange("r (a c) -> (r a) c", a=4)
                    tmp_store[ph] = p.dma("scalar", qts, dst, lst_b[2][:, :])
                    if ph == 1:
                        psT_guard[0] = psT_guard[1] = ev

        if want("A3"):
            p.wait("sync", te, a2_done_t)
            p.wait("sync", qsw[0], p_st)
            wqe3o = W64.rearrange("p (kc n) -> p kc n", n=KV)
            p.dma("sync", qx2, wqe3o, s_wqe.rearrange("kc p n -> p kc n"))
            wqe_loaded = qx2.n
            qev = {}
            first_q = True
            last_mm = None
            for s in range(2):
                if s > 0:
                    p.wait("sync", te, last_mm)
                p.dma("sync", qxn[0], xnbuf3, s_xn[s].rearrange("kc p t -> p kc t"))
                ld = qxn[0].n
                p.wait("tensor", qxn[0], ld)
                p.wait("tensor", qx2, wqe_loaded)
                if first_q:
                    p.wait("tensor", sv, a2_done_v)
                    first_q = False
                for j in range(NH):
                    emit_comb_steps(2)
                    bank = j % 2
                    g = qev.get((s, j - 2)) if j >= 2 else qev.get((s - 1, j + 6))
                    if g is not None:
                        p.wait("tensor", ss, g)
                    mm = None
                    for kc in range(NKC):
                        mm = p.op("tensor", "matmul",
                                  psA[:, bank * 512:(bank + 1) * 512],
                                  W64[:, kc * KV + j * 128: kc * KV + (j + 1) * 128],
                                  xnbuf[:, kc * SCT:(kc + 1) * SCT],
                                  start=(kc == 0), stop=(kc == NKC - 1), sem=te)
                    last_mm = mm
                    p.wait("scalar", te, mm)
                    qev[(s, j)] = p.op("scalar", "activation",
                                       Q_fm[:, j * TOWN + s * SCT: j * TOWN + (s + 1) * SCT],
                                       psA[:, bank * 512:(bank + 1) * 512],
                                       AF.Copy, scale=INVSQ, sem=ss)
            emit_comb_steps(32)
            a3_done_t = te.n
            a3_done_s = ss.n
            if "Q" in tap:
                p.wait("sync", ss, a3_done_s)
                p.dma("sync", qt_, tap["Q"], Q_fm[:, :])

        if want("B"):
            p.wait("scalar", ql[0], lst_ld[30])
            p.wait("scalar", ql[1], lst_ld[31])
            p.wait("scalar", qts, tmp_store[1])
            p.wait("gpsimd", te, a3_done_t)
            wo3o = W64.rearrange("p (kc n) -> p kc n", n=D)
            wo3i = wo.rearrange("(kc p) n -> p kc n", p=128)
            for hv in range(2):
                p.dma("gpsimd", qw, wo3o[:, hv * 4:(hv + 1) * 4], wo3i[:, hv * 4:(hv + 1) * 4])
            wo_loaded = qw.n

            mx = stats[:, 8:9]
            nmx = stats[:, 9:10]
            sexp = stats[:, 10:11]
            rs = stats[:, 11:12]

            exp_done = {"last": a3_done_s}
            scale_done = [a3_done_s, a3_done_s]
            o_ev = {"last": None}
            wo_guard = [None, None]
            woev = {}
            xres_ld = {}
            ln2_done = {}

            def b_scores(qt, h):
                p.wait("tensor", ss, exp_done["last"])
                mm = None
                for n in range(4):
                    mm = p.op("tensor", "matmul",
                              psA[:, n * 512:(n + 1) * 512],
                              Q_fm[:, h * TOWN + qt * 128: h * TOWN + (qt + 1) * 128],
                              K_fm[:, h * TFULL + n * 512: h * TFULL + (n + 1) * 512],
                              start=True, stop=True, sem=te)
                return mm

            def b_softmax(qt, h, sc_done, pbuf):
                p.wait("vector", te, sc_done)
                v = p.op("vector", "tensor_reduce", nmx, psA[:, :], AX.X, ALU.max,
                         negate=True, sem=sv)
                p.wait("scalar", sv, v)
                p.wait("scalar", ss, scale_done[h % 2])
                e = p.op("scalar", "activation", pbuf[:, :], psA[:, :], AF.Exp,
                         bias=nmx, sem=ss, accum_out=sexp)
                exp_done["last"] = e
                p.wait("vector", ss, e)
                rv = p.op("vector", "reciprocal", rs, sexp, sem=sv)
                p.wait("scalar", sv, rv)
                p.wait("scalar", ss, e)
                r = p.op("scalar", "activation", pbuf[:, :], pbuf[:, :], AF.Identity,
                         scale=rs, sem=ss)
                scale_done[h % 2] = r
                return r

            def b_ptrans_and_o(h, ps_done, pbuf):
                mm = None
                for g in range(2):
                    bank = g % 2
                    p.wait("tensor", ss, ps_done)
                    if psT_guard[bank] is not None:
                        p.wait("tensor", sv, psT_guard[bank])
                    tv = None
                    for i in range(8):
                        blk = g * 8 + i
                        tv = p.op("tensor", "transpose",
                                  psT[:, bank * 1024 + i * 128: bank * 1024 + (i + 1) * 128],
                                  pbuf[:, blk * 128:(blk + 1) * 128], idn[:, :], sem=te)
                    p.wait("vector", te, tv)
                    dst = PT_sb.rearrange("p (b d) -> p b d", d=128)[:, g * 8:(g + 1) * 8]
                    psT_guard[bank] = p.op("vector", "tensor_copy", dst, psT4[:, bank], sem=sv)
                p.wait("tensor", sv, psT_guard[1])
                if o_ev["last"] is not None:
                    p.wait("tensor", ss, o_ev["last"])
                if wo_guard[1] is not None:
                    p.wait("tensor", sv, wo_guard[1])
                for blk in range(16):
                    mm = p.op("tensor", "matmul", psO[:, 0:128],
                              V_tm[:, blk * KV + h * 128: blk * KV + (h + 1) * 128],
                              PT_sb[:, blk * 128:(blk + 1) * 128],
                              start=(blk == 0), stop=(blk == 15), sem=te)
                p.wait("scalar", te, mm)
                o_ev["last"] = p.op("scalar", "activation", O_fm[:, h * 128:(h + 1) * 128],
                                    psO[:, 0:128], AF.Copy, sem=ss)

            xnew_st = {}
            ln2_trans = {}

            def emit_xres_load(qt):
                if qt >= 1:
                    p.wait("sync", sv, woev[qt - 1])
                    p.wait("sync", ss, ln2_done[qt - 1])
                    p.wait("sync", qt_, xnew_st[qt - 1])
                    p.wait("sync", te, ln2_trans[qt - 1])
                p.dma("sync", qs, xresA[:, :], x_own[qt * 128:(qt + 1) * 128, 0:2048])
                p.dma("sync", qs, xresB[:, :], x_own[qt * 128:(qt + 1) * 128, 2048:4096])
                xres_ld[qt] = qs.n

            def emit_wo(qt):
                p.wait("tensor", ss, o_ev["last"])
                p.wait("tensor", qw, wo_loaded)
                wops = [psW, psO]
                for cb in range(8):
                    bank = wops[cb % 2]
                    if wo_guard[cb % 2] is not None:
                        p.wait("tensor", sv, wo_guard[cb % 2])
                    mm = None
                    for kc in range(NH):
                        mm = p.op("tensor", "matmul", bank[:, 0:512],
                                  O_fm[:, kc * 128:(kc + 1) * 128],
                                  W64[:, kc * D + cb * 512: kc * D + (cb + 1) * 512],
                                  start=(kc == 0), stop=(kc == NH - 1), sem=te)
                    p.wait("vector", te, mm)
                    if cb == 0:
                        p.wait("vector", qs, xres_ld[qt])
                    xr = xresA if cb < 4 else xresB
                    co = (cb % 4) * 512
                    wo_guard[cb % 2] = p.op("vector", "tensor_add",
                                            xr[:, co:co + 512], xr[:, co:co + 512],
                                            bank[:, 0:512], sem=sv)
                woev[qt] = wo_guard[1]
                p.wait("sync", sv, woev[qt])
                p.dma("sync", qt_, s_xnew[qt * 128:(qt + 1) * 128, 0:2048], xresA[:, :])
                xnew_st[qt] = p.dma("sync", qt_, s_xnew[qt * 128:(qt + 1) * 128, 2048:4096],
                                    xresB[:, :])

            def emit_ln2_stats(qt):
                p.wait("scalar", sv, woev[qt])
                p.wait("vector", sv, woev[qt])
                p.wait("scalar", qt_, xnew_st[qt])
                if qt >= 1:
                    p.wait("scalar", te, ln2_trans[qt - 1])
                ln2_done[qt] = ln_chunk([(xresA[:, :], xn2dA[:, :]),
                                         (xresB[:, :], xn2dB[:, :])])

            def emit_ln2_trans(qt):
                s2, c2 = divmod(qt, 4)
                dumps = [xn2dA, xn2dB]
                for gg in range(4):
                    bank = gg % 2
                    p.wait("tensor", ss, ln2_done[qt])
                    if c2 == 0 and s2 >= 1 and gg == 0:
                        p.wait("vector", qxs, xn2_store[s2 - 1])
                    if psT_guard[bank] is not None:
                        p.wait("tensor", sv, psT_guard[bank])
                    tv = None
                    for i in range(8):
                        kc = gg * 8 + i
                        src = dumps[kc // 16]
                        tv = p.op("tensor", "transpose",
                                  psT[:, bank * 1024 + i * 128: bank * 1024 + (i + 1) * 128],
                                  src[:, (kc % 16) * 128:(kc % 16 + 1) * 128],
                                  idn[:, :], sem=te)
                    p.wait("vector", te, tv)
                    dst = xnbuf3[:, gg * 8:(gg + 1) * 8, c2 * 128:(c2 + 1) * 128]
                    psT_guard[bank] = p.op("vector", "tensor_copy", dst, psT4[:, bank], sem=sv)
                ln2_trans[qt] = te.n
                if c2 == 3:
                    p.wait("sync", sv)
                    xn2_store[s2] = p.dma("sync", qxs,
                                          s_xn2[s2].rearrange("kc p t -> p kc t"), xnbuf3)

            emit_xres_load(0)
            pend = None
            for qt in range(NQT):
                for h in range(NH):
                    sc = b_scores(qt, h)
                    if qt >= 1 and h == 1:
                        emit_wo(qt - 1)
                    if pend is not None:
                        b_ptrans_and_o(*pend)
                    pend_ps = b_softmax(qt, h, sc, P_tm[h % 2])
                    if qt == 0 and h == 0 and "P0" in tap:
                        p.wait("sync", ss, pend_ps)
                        p.dma("sync", qt_, tap["P0"], P_tm[0][:, :])
                        p.wait("scalar", qt_)
                    pend = (h, pend_ps, P_tm[h % 2])
                    if qt >= 1:
                        if h == 2:
                            emit_ln2_stats(qt - 1)
                        elif h == 5:
                            emit_ln2_trans(qt - 1)
                        elif h == 6:
                            emit_xres_load(qt)
            b_ptrans_and_o(*pend)
            emit_wo(NQT - 1)
            emit_ln2_stats(NQT - 1)
            emit_ln2_trans(NQT - 1)
            b_done_t = te.n
            b_done_v = sv.n
            b_done_s = ss.n
            if "XNEW" in tap:
                p.wait("sync", qt_)
                p.dma("sync", qt_, tap["XNEW"], s_xnew)
            if "XN2" in tap:
                p.wait("sync", qt_)
                p.wait("sync", qxs)
                for s2 in range(2):
                    p.dma("sync", qt_, tap["XN2"][s2], s_xn2[s2])

        if want("C"):
            xnew_stores = qt_.n

            k3o = ker_bf.rearrange("p (kc n) -> p kc n", n=2048)
            q_loaded = {}
            ker_free = {}

            def emit_ker_quarter(hf, q):
                if hf == 0 and q == 0:
                    p.wait("gpsimd", te, b_done_t)
                    p.wait("gpsimd", ss, b_done_s)
                    p.wait("gpsimd", sv, b_done_v)
                    p.wait("gpsimd", qt_, xnew_stores)
                if hf == 1:
                    p.wait("gpsimd", te, ker_free[q])
                k3i = ker.rearrange("(kc p) n -> p kc n", p=128)[:, :, hf * 2048:(hf + 1) * 2048]
                q_loaded[(hf, q)] = p.dma("gpsimd", qk[q], k3o[:, q * 8:(q + 1) * 8],
                                          k3i[:, q * 8:(q + 1) * 8])

            for q in range(4):
                emit_ker_quarter(0, q)
            p.wait("gpsimd", qts, tmp_store[1])
            acat3 = A_catT.rearrange("p (kc r) -> p kc r", r=R)
            p.dma("gpsimd", ql[0], acat3[:, :, 0:32], s_tmpa.rearrange("(kc p) r -> p kc r", p=128))
            p.dma("gpsimd", ql[0], acat3[:, :, 32:64], last.rearrange("(kc p) r -> p kc r", p=128))
            p.dma("gpsimd", ql[0], B_catT[0:32, :], s_tmpb)
            p.dma("gpsimd", ql[0], B_catT[32:64, :], lbst)
            lora_ready = ql[0].n

            woev2 = {}
            out_st = {}
            kermm_done = None
            ax_ready = {}
            loaded_s2 = [None]
            idx = 0
            for hf in range(2):
                if hf == 0:
                    p.wait("sync", te, b_done_t)
                    p.wait("sync", ss, b_done_s)
                    p.wait("sync", sv, b_done_v)
                    p.wait("sync", qt_, xnew_stores)
                p.dma("sync", qsw[hf], bias_h[:, :], bias_b[:, hf * 2048:(hf + 1) * 2048])
                bias_ld = qsw[hf].n
                for s2 in ((0, 1) if hf == 0 else (1, 0)):
                    if loaded_s2[0] != s2:
                        if (hf, s2) != (0, 0):
                            p.wait("sync", te, kermm_done)
                        if hf == 0 and want("B"):
                            p.wait("sync", qxs, max(xn2_store.values()))
                        p.dma("sync", qx2, xnbuf3, s_xn2[s2].rearrange("kc p t -> p kc t"))
                        xn2_ld = qx2.n
                        loaded_s2[0] = s2
                    for t in range(4):
                        chunk = s2 * 4 + t
                        g = woev2.get(idx - 2)
                        if g is not None:
                            p.wait("sync", sv, g)
                            p.wait("sync", qo, 16 * idx)
                        p.dma("sync", qb[idx % 2], xnb[idx % 2][:, :],
                              s_xnew[chunk * 128:(chunk + 1) * 128, hf * 2048:(hf + 1) * 2048])
                        xnew_ld = qb[idx % 2].n
                        p.wait("tensor", qx2, xn2_ld)
                        g = woev2.get(idx - 1)
                        if g is not None:
                            p.wait("tensor", sv, g)
                        if idx == 0:
                            p.wait("tensor", ss, b_done_s)
                        mm = None
                        for kc in range(NKC):
                            if kc % 8 == 0:
                                p.wait("tensor", qk[kc // 8], q_loaded[(hf, kc // 8)])
                            lhs = xnbuf[:, kc * SCT + t * 128: kc * SCT + (t + 1) * 128]
                            for cb in range(4):
                                mm = p.op("tensor", "matmul", psA[:, cb * 512:(cb + 1) * 512],
                                          lhs,
                                          ker_bf[:, kc * 2048 + cb * 512: kc * 2048 + (cb + 1) * 512],
                                          start=(kc == 0), stop=False, sem=te)
                            if hf == 0 and idx == 7 and kc % 8 == 7:
                                ker_free[kc // 8] = te.n
                                emit_ker_quarter(1, kc // 8)
                        if hf == 0 and t == 0:
                            p.wait("tensor", ql[0], lora_ready)
                            if s2 == 0:
                                p.wait("tensor", sv, b_done_v)
                            else:
                                p.wait("tensor", ss, ax_ready[1 - s2])
                            axm = None
                            for kc in range(NKC):
                                axm = p.op("tensor", "matmul", psW[0:64, 0:512],
                                           A_catT[:, kc * R:(kc + 1) * R],
                                           xnbuf[:, kc * SCT:(kc + 1) * SCT],
                                           start=(kc == 0), stop=(kc == NKC - 1), sem=te)
                            p.wait("scalar", te, axm)
                            if s2 == 0:
                                p.wait("scalar", qt_, xnew_stores)
                            ax_ready[s2] = p.op("scalar", "activation",
                                                Ax_bf[:, s2 * SCT:(s2 + 1) * SCT],
                                                psW[0:64, 0:512], AF.Copy, sem=ss)
                        p.wait("tensor", ss, ax_ready[s2])
                        for cb in range(4):
                            mm = p.op("tensor", "matmul", psA[:, cb * 512:(cb + 1) * 512],
                                      Ax_bf[:, chunk * 128:(chunk + 1) * 128],
                                      B_catT[:, hf * 2048 + cb * 512: hf * 2048 + (cb + 1) * 512],
                                      start=False, stop=True, sem=te)
                        kermm_done = mm
                        p.wait("vector", te, kermm_done)
                        p.wait("vector", qb[idx % 2], xnew_ld)
                        p.wait("vector", qsw[hf], bias_ld)
                        av = None
                        for cb in range(4):
                            av = p.op("vector", "tensor_add",
                                      xnb[idx % 2][:, cb * 512:(cb + 1) * 512],
                                      xnb[idx % 2][:, cb * 512:(cb + 1) * 512],
                                      psA[:, cb * 512:(cb + 1) * 512], sem=sv)
                        p.wait("vector", sv, av)
                        woev2[idx] = p.op("vector", "tensor_add", xnb[idx % 2][:, :],
                                          xnb[idx % 2][:, :], bias_h[:, :], sem=sv)
                        p.wait("sync", sv, woev2[idx])
                        out_st[idx] = p.dma("sync", qo,
                                            out[chunk * 128:(chunk + 1) * 128,
                                                hf * 2048:(hf + 1) * 2048],
                                            xnb[idx % 2][:, :])
                        idx += 1

        p.wait("sync", qt_)
        p.wait("sync", qo)
        p.wait("sync", qxs)

        with nc.Block() as block:
            @block.sync
            def _(e):
                for f in p.q["sync"]:
                    f(e)

            @block.scalar
            def _(e):
                for f in p.q["scalar"]:
                    f(e)

            @block.vector
            def _(e):
                for f in p.q["vector"]:
                    f(e)

            @block.tensor
            def _(e):
                for f in p.q["tensor"]:
                    f(e)

            @block.gpsimd
            def _(e):
                for f in p.q["gpsimd"]:
                    f(e)

    return nc


def make_in_maps(inputs):
    X = np.asarray(inputs["L0_attn_x"], dtype=np.float32)
    wq = np.ascontiguousarray(np.asarray(inputs["L0_attn_wq"], np.float32))
    wk = np.ascontiguousarray(np.asarray(inputs["L0_attn_wk"], np.float32))
    wv = np.ascontiguousarray(np.asarray(inputs["L0_attn_wv"], np.float32))
    wo = np.ascontiguousarray(np.asarray(inputs["L0_attn_wo"], np.float32))
    gate = np.asarray(inputs["L0_ffn_gate"], np.float32)
    ker = np.ascontiguousarray(np.asarray(inputs["L0_ffn_kernel"], np.float32))
    bias = np.asarray(inputs["L0_ffn_bias"], np.float32)
    lA = np.asarray(inputs["L0_ffn_lora_A"], np.float32)
    lB = np.asarray(inputs["L0_ffn_lora_B"], np.float32)
    lAs = np.asarray(inputs["L0_ffn_lora_A_shared"], np.float32)
    lBs = np.asarray(inputs["L0_ffn_lora_B_shared"], np.float32)

    lat = np.ascontiguousarray(lA.transpose(0, 2, 1)).reshape(16, 128, 1024)
    lbt = np.ascontiguousarray(lB.transpose(0, 2, 1)).reshape(16, 128, 1024)
    last = np.ascontiguousarray(lAs.T)
    lbst = np.ascontiguousarray(lBs.T)
    bias_b = np.ascontiguousarray(np.broadcast_to(bias, (128, 4096)))

    in_maps = []
    for c in range(8):
        b, h = divmod(c, 2)
        gate_b = np.ascontiguousarray(np.broadcast_to(gate[2 * b + h], (128, 16)))
        in_maps.append({
            "x_own": np.ascontiguousarray(X[b, h * 1024:(h + 1) * 1024]),
            "x_oth": np.ascontiguousarray(X[b, (1 - h) * 1024:(2 - h) * 1024]),
            "wq": wq, "wk": wk, "wv": wv, "wo": wo, "ker": ker,
            "bias_b": bias_b, "gate_b": gate_b,
            "lat": lat, "lbt": lbt, "last": last, "lbst": lbst,
        })
    return in_maps


def kernel(**inputs):
    nc = build_nc()
    in_maps = make_in_maps(inputs)
    res = run_bass_kernel_spmd(nc, in_maps, core_ids=list(range(8)))
    out = np.empty((4, 2048, 4096), np.float32)
    for c, r in enumerate(res.results):
        b, h = divmod(c, 2)
        out[b, h * 1024:(h + 1) * 1024] = r["out"]
    return out


# version 20
# speedup vs baseline: 1.0670x; 1.0259x over previous
import contextlib
import numpy as np
import ml_dtypes
import concourse.bass as bass
import concourse.mybir as mybir
from concourse.bass_utils import run_bass_kernel_spmd

F32 = mybir.dt.float32
BF16 = mybir.dt.bfloat16
AF = mybir.ActivationFunctionType
ALU = mybir.AluOpType
AX = mybir.AxisListType

D = 4096
KV = 1024
NH = 8
NKC = 32
TOWN = 1024
TFULL = 2048
NSC = 4
SCT = 512
NQT = 8
R = 64
INVSQ = float(1.0 / np.sqrt(128.0))
EPS = 1e-5

SBUF_BASE = 16512
KiB = 1024

ENGINES = ("sync", "scalar", "vector", "tensor", "gpsimd")


class Sem:
    __slots__ = ("h", "n")

    def __init__(self, h):
        self.h = h
        self.n = 0


class P:

    def __init__(self, nc):
        self.nc = nc
        self.q = {e: [] for e in ENGINES}

    def emit(self, eng, fn):
        self.q[eng].append(fn)

    def dma(self, eng, sem, out, in_):
        sem.n += 16
        self.emit(eng, lambda e, o=out, i=in_, h=sem.h: e.dma_start(out=o, in_=i).then_inc(h, 16))
        return sem.n

    def wait(self, eng, sem, val=None):
        v = sem.n if val is None else val
        if v is None or v <= 0:
            return
        self.emit(eng, lambda e, h=sem.h, v=v: e.wait_ge(h, v))

    def op(self, eng, name, *args, sem=None, **kw):
        if sem is not None:
            sem.n += 1
        s = sem

        def f(e, name=name, args=args, kw=kw, s=s):
            ins = getattr(e, name)(*args, **kw)
            if s is not None:
                ins.then_inc(s.h, 1)
        self.emit(eng, f)
        return None if s is None else s.n


def build_nc(taps=(), upto="C"):
    nc = bass.Bass("TRN2", target_bir_lowering=False)
    phase_order = ["P", "A1", "A2", "A3", "B", "C"]
    upto_i = phase_order.index(upto)

    def want(ph):
        return phase_order.index(ph) <= upto_i

    x_own = nc.dram_tensor("x_own", [TOWN, D], F32, kind="ExternalInput").ap()
    x_oth = nc.dram_tensor("x_oth", [TOWN, D], F32, kind="ExternalInput").ap()
    wq = nc.dram_tensor("wq", [D, D], F32, kind="ExternalInput").ap()
    wk = nc.dram_tensor("wk", [D, KV], F32, kind="ExternalInput").ap()
    wv = nc.dram_tensor("wv", [D, KV], F32, kind="ExternalInput").ap()
    wo = nc.dram_tensor("wo", [KV, D], F32, kind="ExternalInput").ap()
    ker = nc.dram_tensor("ker", [D, D], F32, kind="ExternalInput").ap()
    bias_b = nc.dram_tensor("bias_b", [128, D], F32, kind="ExternalInput").ap()
    gate_bi = nc.dram_tensor("gate_b", [128, 16], F32, kind="ExternalInput").ap()
    lat = nc.dram_tensor("lat", [16, 128, 1024], F32, kind="ExternalInput").ap()
    lbt = nc.dram_tensor("lbt", [16, 128, 1024], F32, kind="ExternalInput").ap()
    last = nc.dram_tensor("last", [D, 32], F32, kind="ExternalInput").ap()
    lbst = nc.dram_tensor("lbst", [32, D], F32, kind="ExternalInput").ap()

    out = nc.dram_tensor("out", [TOWN, D], F32, kind="ExternalOutput").ap()

    s_wqe = nc.dram_tensor("s_wqe", [NKC, 128, KV], BF16, kind="Internal").ap()
    s_xn = nc.dram_tensor("s_xn", [NSC, NKC, 128, SCT], BF16, kind="Internal").ap()
    s_xn2 = nc.dram_tensor("s_xn2", [2, NKC, 128, SCT], BF16, kind="Internal").ap()
    s_xnew = nc.dram_tensor("s_xnew", [TOWN, D], F32, kind="Internal").ap()
    s_tmpa = nc.dram_tensor("s_tmpa", [D, 32], BF16, kind="Internal").ap()
    s_tmpb = nc.dram_tensor("s_tmpb", [32, D], BF16, kind="Internal").ap()

    ident = nc.inline_tensor(np.eye(128, dtype=ml_dtypes.bfloat16), name="ident").ap()

    tap = {}
    for name, shape, dt in [
        ("K", [128, NH * TFULL], BF16), ("V", [128, 16 * KV], BF16),
        ("Q", [128, NH * TOWN], BF16), ("XN", [NSC, NKC, 128, SCT], BF16),
        ("XNEW", [TOWN, D], F32), ("XN2", [2, NKC, 128, SCT], BF16),
        ("WQE", [NKC, 128, KV], BF16),
        ("P0", [128, TFULL], BF16), ("O0", [128, NH * 128], BF16),
        ("XR0", [128, D], F32), ("PT0", [128, 16 * 128], BF16),
    ]:
        if name in taps:
            tap[name] = nc.dram_tensor("tap_" + name, shape, dt, kind="ExternalOutput").ap()

    def sb(name, shape, dt, off_kib):
        return nc.alloc_sbuf_tensor_at(name, shape, dt, offset=SBUF_BASE + int(off_kib * KiB)).ap()

    xnbuf = sb("xnbuf", [128, NKC * SCT], BF16, 0)
    V_tm = sb("V_tm", [128, 16 * KV], BF16, 32)
    xbuf = [sb("xbuf0", [128, D], F32, 32),
            sb("xbuf1", [128, D], F32, 48)]
    xn_tm = sb("xn_tm", [128, D], BF16, 64)
    xresA = sb("xresA", [128, 2048], F32, 64)
    Q_fm = sb("Q_fm", [128, NH * TOWN], BF16, 72)
    xnsA = sb("xnsA", [128, 16 * SCT], BF16, 72)
    xnsB = sb("xnsB", [128, 8 * SCT], BF16, 88)
    P_tm = [sb("P_tm0", [128, TFULL], BF16, 88),
            sb("P_tm1", [128, TFULL], BF16, 92)]
    xn2dA = sb("xn2dA", [128, 2048], BF16, 64)
    xn2dB = sb("xn2dB", [128, 2048], BF16, 192)
    W64 = sb("W64", [128, NKC * KV], BF16, 96)
    K_fm = sb("K_fm", [128, NH * TFULL], BF16, 160)
    xnsC = sb("xnsC", [128, 8 * SCT], BF16, 192)
    xresB = sb("xresB", [128, 2048], F32, 192)
    wqb = [sb("wqb0", [128, D], BF16, 72), sb("wqb1", [128, D], BF16, 80)]
    ptmpb = sb("ptmpb", [128, KV], BF16, 88)
    wqecP = sb("wqecP", [128, 2 * KV], BF16, 90)
    wqec = [wqecP[:, 0:KV], wqecP[:, KV:2 * KV]]
    ker_bf = sb("ker_bf", [128, NKC * 2048], BF16, 32)
    xnb = [sb("xnb0", [128, 2048], F32, 160), sb("xnb1", [128, 2048], F32, 168)]
    bias_h = sb("bias_h", [128, 2048], F32, 176)
    A_catT = sb("A_catT", [128, NKC * R], BF16, 184)
    B_catT = sb("B_catT", [R, D], BF16, 188)
    Ax_bf = sb("Ax_bf", [R, TOWN], BF16, 196)
    idn = sb("idn", [128, 128], BF16, 200)
    gate_b = sb("gate_sb", [128, 16], F32, 200.25)
    stats = sb("stats", [128, 64], F32, 200.5)
    ceps = sb("ceps", [128, 1], F32, 200.40625)
    O_fm = sb("O_fm", [128, NH * 128], BF16, 201)
    PT_sb = sb("PT_sb", [128, 16 * 128], BF16, 203)
    lst_b = [sb(f"lstb{i}", [128, KV], BF16, 88 + 2 * i) for i in range(3)]

    psA = nc.alloc_psum_tensor("psA", [128, 4 * 512], F32).ap()
    psT = nc.alloc_psum_tensor("psT", [128, 2048], BF16).ap()
    psO = nc.alloc_psum_tensor("psO", [128, 512], F32).ap()
    psW = nc.alloc_psum_tensor("psW", [128, 512], F32).ap()

    p = P(nc)
    xnbuf3 = xnbuf.rearrange("p (kc t) -> p kc t", t=SCT)
    psT4 = psT.rearrange("p (b i d) -> p b i d", b=2, d=128)

    with contextlib.ExitStack() as stx:
        mk = lambda nm: Sem(stx.enter_context(nc.semaphore(nm)))
        qs = mk("qs")
        qt_ = mk("qt")
        qg = mk("qg")
        sv = mk("sv")
        ss = mk("ss")
        te = mk("te")
        qw = mk("qw")
        qp = [mk("qp0"), mk("qp1")]
        qsw = [mk("qsw0"), mk("qsw1")]
        qxn = [mk("qxn0"), mk("qxn1")]
        qk = [mk(f"qk{i}") for i in range(4)]
        ql = [mk("ql0"), mk("ql1")]
        qts = mk("qts")
        qb = [mk("qb0"), mk("qb1")]
        qx2 = mk("qx2")
        qx = [mk("qx0"), mk("qx1")]
        qxs = mk("qxs")
        qo = mk("qo")

        psT_guard = [None, None]

        p.dma("sync", qs, idn[:, :], ident)
        p.dma("sync", qs, gate_b[:, :], gate_bi)
        eps_set = p.op("vector", "memset", ceps[:, :], EPS, sem=sv)
        p.wait("scalar", sv, eps_set)

        a1_done_v = a1_done_t = a2_done_v = a2_done_t = None
        a3_done_t = a3_done_s = b_done_t = b_done_v = b_done_s = None
        xn_store = {}
        xn2_store = {}

        presum_done = {}
        wqe_load = {}
        wqe_stored = {}

        def emit_presum_load(kc):
            buf = wqb[kc % 2]
            if kc >= 2:
                p.wait("gpsimd", sv, presum_done[kc - 2])
            wqe_load[kc] = p.dma("gpsimd", qp[kc % 2], buf[:, :], wq[kc * 128:(kc + 1) * 128, :])

        def emit_presum_compute(kc):
            buf = wqb[kc % 2]
            p.wait("vector", qp[kc % 2], wqe_load[kc])
            if kc >= 1:
                p.wait("vector", sv, presum_done[kc - 1])
            v4 = buf.rearrange("p (h g d) -> p g h d", g=4, d=128)
            pt3 = ptmpb.rearrange("p (h d) -> p h d", d=128)
            t1 = p.op("vector", "tensor_add", pt3, v4[:, 0], v4[:, 1], sem=sv)
            p.wait("vector", sv, t1)
            t2 = p.op("vector", "tensor_add", pt3, pt3, v4[:, 2], sem=sv)
            p.wait("vector", sv, t2)
            wc = wqec[kc % 2]
            if kc >= 2:
                g = wqe_stored.get(kc - 1 if kc % 2 == 0 else kc - 2)
                if g is not None:
                    p.wait("vector", qsw[0], g)
            presum_done[kc] = p.op("vector", "tensor_add",
                                   wc.rearrange("p (h d) -> p h d", d=128),
                                   pt3, v4[:, 3], sem=sv)
            if kc % 2 == 1:
                p.wait("sync", sv, presum_done[kc])
                wqe_stored[kc] = p.dma(
                    "sync", qsw[0],
                    s_wqe[kc - 1:kc + 1].rearrange("two p n -> p two n"),
                    wqecP.rearrange("p (two n) -> p two n", two=2))

        def ln_chunk(pieces):
            mean, var, tvar, tsd, rstd, negmr = (stats[:, i:i + 1] for i in range(6))
            mv = stats[:, 0:2]
            k = 0
            bl = None
            for src, dump in pieces:
                w = (D // len(pieces)) // 512
                for t0 in range(w):
                    bl = p.op("vector", "bn_stats", stats[:, 16 + 6 * k:22 + 6 * k],
                              src[:, t0 * 512:(t0 + 1) * 512], sem=sv)
                    k += 1
            p.wait("vector", sv, bl)
            ag = p.op("vector", "bn_aggr", mv, stats[:, 16:16 + 6 * k], sem=sv)
            p.wait("vector", sv, ag)
            v2 = p.op("vector", "tensor_scalar", tvar, var, 1.0, EPS,
                      ALU.mult, ALU.add, sem=sv)
            p.wait("scalar", sv, v2)
            s3 = p.op("scalar", "activation", tsd, tvar, AF.Sqrt, sem=ss)
            p.wait("vector", ss, s3)
            rv = p.op("vector", "reciprocal", rstd, tsd, sem=sv)
            p.wait("vector", sv, rv)
            v3 = p.op("vector", "tensor_scalar", negmr, mean, rstd, -1.0,
                      ALU.mult, ALU.mult, sem=sv)
            p.wait("scalar", sv, v3)
            r = None
            for src, dump in pieces:
                r = p.op("scalar", "activation", dump, src, AF.Identity,
                         scale=rstd, bias=negmr, sem=ss)
            return r

        def transpose_chunk(dump_bf, dst3, c, norm_done):
            tv = ev = None
            for g in range(4):
                bank = g % 2
                p.wait("tensor", ss, norm_done)
                if psT_guard[bank] is not None:
                    p.wait("tensor", sv, psT_guard[bank])
                for i in range(8):
                    kc = g * 8 + i
                    tv = p.op("tensor", "transpose",
                              psT[:, bank * 1024 + i * 128: bank * 1024 + (i + 1) * 128],
                              dump_bf[:, kc * 128:(kc + 1) * 128], idn[:, :], sem=te)
                p.wait("vector", te, tv)
                dst = dst3[:, g * 8:(g + 1) * 8, c * 128:(c + 1) * 128]
                ev = p.op("vector", "tensor_copy", dst, psT4[:, bank], sem=sv)
                psT_guard[bank] = ev
            return tv, ev

        if want("A1"):
            p.wait("tensor", qs, 32)
            emit_presum_load(0)
            emit_presum_load(1)
            wk3o = W64.rearrange("p (kc n) -> p kc n", n=KV)
            wk3i = wk.rearrange("(kc p) n -> p kc n", p=128)
            for qtr in range(4):
                p.dma("gpsimd", qw, wk3o[:, qtr * 8:(qtr + 1) * 8], wk3i[:, qtr * 8:(qtr + 1) * 8])
            wk_loaded = qw.n

            kev = {}
            kmm_done = {}
            ln_done = {}
            trans_done = {}
            for i in range(16):
                s, c = divmod(i, 4)
                src = x_own if i < 8 else x_oth
                row = (i % 8) * 128
                if i >= 2:
                    p.wait("scalar", ss, ln_done[i - 2])
                xld = p.dma("scalar", qx[i % 2], xbuf[i % 2][:, :], src[row:row + 128, :])
                p.wait("scalar", qx[i % 2], xld)
                p.wait("vector", qx[i % 2], xld)
                if i >= 1:
                    p.wait("scalar", te, trans_done[i - 1])
                ln_done[i] = ln_chunk([(xbuf[i % 2][:, :], xn_tm[:, :])])
                if i > 0:
                    emit_presum_compute(2 * (i - 1))
                    emit_presum_compute(2 * (i - 1) + 1)
                    emit_presum_load(2 * i)
                    emit_presum_load(2 * i + 1)
                if c == 0 and s >= 1:
                    p.wait("vector", te, kmm_done[s - 1])
                    p.wait("vector", qt_, xn_store[s - 1])
                trans_done[i], _ = transpose_chunk(xn_tm[:, :], xnbuf3, c, ln_done[i])
                if c == 3:
                    p.wait("sync", sv)
                    xn_store[s] = p.dma("sync", qt_, s_xn[s].rearrange("kc p t -> p kc t"),
                                        xnbuf3)
                    p.wait("tensor", sv)
                    p.wait("tensor", qw, wk_loaded)
                    for j in range(NH):
                        bank = j % 2
                        if j >= 2 and kev.get(j - 2) is not None:
                            p.wait("tensor", sv, kev[j - 2])
                        mm = None
                        for kc in range(NKC):
                            mm = p.op("tensor", "matmul",
                                      psA[:, bank * 512:(bank + 1) * 512],
                                      W64[:, kc * KV + j * 128: kc * KV + (j + 1) * 128],
                                      xnbuf[:, kc * SCT:(kc + 1) * SCT],
                                      start=(kc == 0), stop=(kc == NKC - 1), sem=te)
                        kmm_done[s] = mm
                        p.wait("vector", te, mm)
                        kev[j] = p.op("vector", "tensor_copy",
                                      K_fm[:, j * TFULL + s * SCT: j * TFULL + (s + 1) * SCT],
                                      psA[:, bank * 512:(bank + 1) * 512], sem=sv)
            emit_presum_compute(NKC - 2)
            emit_presum_compute(NKC - 1)
            a1_done_v = sv.n
            a1_done_t = te.n
            p_done_v = presum_done[NKC - 1]
            p_st = wqe_stored[NKC - 1]
            if "WQE" in tap:
                p.wait("sync", qsw[0], p_st)
                for kc in range(NKC):
                    p.dma("sync", qt_, tap["WQE"][kc], s_wqe[kc])
            if "XN" in tap:
                p.wait("sync", qt_)
                for s in range(NSC):
                    p.dma("sync", qt_, tap["XN"][s], s_xn[s])
            if "K" in tap:
                p.wait("sync", sv, a1_done_v)
                p.dma("sync", qt_, tap["K"], K_fm[:, :])

        if want("A2"):
            p.wait("gpsimd", te, a1_done_t)
            wv3o = W64.rearrange("p (kc n) -> p kc n", n=KV)
            wv3i = wv.rearrange("(kc p) n -> p kc n", p=128)
            for qtr in range(4):
                p.dma("gpsimd", qw, wv3o[:, qtr * 8:(qtr + 1) * 8], wv3i[:, qtr * 8:(qtr + 1) * 8])
            wv_loaded = qw.n

            def xn_view(par, kc):
                if par == 0:
                    return xnbuf[:, kc * SCT:(kc + 1) * SCT]
                if kc < 16:
                    return xnsA[:, kc * SCT:(kc + 1) * SCT]
                if kc < 24:
                    return xnsB[:, (kc - 16) * SCT:(kc - 15) * SCT]
                return xnsC[:, (kc - 24) * SCT:(kc - 23) * SCT]

            def xn_load(par, s, guard_te):
                p.wait("sync", qt_, xn_store[NSC - 1] if par == 0 else xn_store[s])
                if guard_te is not None:
                    p.wait("sync", te, guard_te)
                if par != 0:
                    p.wait("sync", sv, p_done_v)
                    p.wait("sync", qsw[0], p_st)
                src = s_xn[s].rearrange("kc p t -> p kc t")
                if par == 0:
                    p.dma("sync", qxn[0], xnbuf3, src)
                else:
                    p.dma("sync", qxn[1], xnsA.rearrange("p (kc t) -> p kc t", t=SCT), src[:, 0:16])
                    p.dma("sync", qxn[1], xnsB.rearrange("p (kc t) -> p kc t", t=SCT), src[:, 16:24])
                    p.dma("sync", qxn[1], xnsC.rearrange("p (kc t) -> p kc t", t=SCT), src[:, 24:32])
                return qxn[par].n

            vload = {}
            vev = {}
            vmm_done = {}
            vload[0] = xn_load(0, 0, a1_done_t)
            first_vmm = True
            for s in range(NSC):
                par = s % 2
                if s + 1 < NSC:
                    vload[s + 1] = xn_load((s + 1) % 2, s + 1, vmm_done.get(s - 1))
                p.wait("tensor", qxn[s % 2], vload[s])
                p.wait("tensor", qw, wv_loaded)
                if first_vmm:
                    p.wait("tensor", sv, a1_done_v)
                    first_vmm = False
                for t in range(4):
                    blk = s * 4 + t
                    b0 = 0 if t % 2 == 0 else 2
                    for n0 in (0, 1):
                        g = vev.get((blk - 2, n0))
                        if g is not None:
                            p.wait("tensor", sv, g)
                    mm = None
                    for kc in range(NKC):
                        lhs = xn_view(par, kc)[:, t * 128:(t + 1) * 128]
                        for n0 in (0, 1):
                            bank = b0 + n0
                            mm = p.op("tensor", "matmul",
                                      psA[:, bank * 512:(bank + 1) * 512],
                                      lhs, W64[:, kc * KV + n0 * 512: kc * KV + (n0 + 1) * 512],
                                      start=(kc == 0), stop=(kc == NKC - 1), sem=te)
                    vmm_done[s] = mm
                    p.wait("vector", te, mm)
                    for n0 in (0, 1):
                        bank = b0 + n0
                        vev[(blk, n0)] = p.op(
                            "vector", "tensor_copy",
                            V_tm[:, blk * KV + n0 * 512: blk * KV + (n0 + 1) * 512],
                            psA[:, bank * 512:(bank + 1) * 512], sem=sv)
            a2_done_v = sv.n
            a2_done_t = te.n
            if "V" in tap:
                p.wait("sync", sv, a2_done_v)
                p.dma("sync", qt_, tap["V"], V_tm[:, :])

        accP = psT.bitcast(F32)
        lat2 = [lat, lbt]
        comb_done = {}
        lst_ld = {}
        tmp_store = [None, None]
        acc_evac = [None, None]
        comb_k = [0]
        nlst = 2

        def emit_comb_steps(n_steps):
            for _ in range(n_steps):
                k = comb_k[0]
                if k >= 32:
                    return
                comb_k[0] += 1
                ph, n = divmod(k, 16)
                buf = lst_b[k % nlst]
                if k == 0:
                    p.wait("gpsimd", te, a2_done_t)
                    p.wait("gpsimd", sv, p_done_v)
                    p.wait("gpsimd", qsw[0], p_st)
                if k >= nlst:
                    p.wait("gpsimd", sv, comb_done[k - nlst])
                lst_ld[k] = p.dma("gpsimd", ql[k % 2], buf[:, :], lat2[ph][n])
                p.wait("vector", ql[k % 2], lst_ld[k])
                if n == 0:
                    if ph == 0:
                        p.wait("vector", qs, 32)
                        for bank in range(2):
                            if psT_guard[bank] is not None:
                                p.wait("vector", sv, psT_guard[bank])
                    else:
                        p.wait("vector", sv, acc_evac[0])
                    comb_done[k] = p.op("vector", "tensor_scalar", accP[:, :],
                                        buf[:, :], gate_b[:, 0:1], None,
                                        ALU.mult, sem=sv)
                else:
                    comb_done[k] = p.op("vector", "scalar_tensor_tensor",
                                        accP[:, :], buf[:, :],
                                        gate_b[:, n:n + 1], accP[:, :],
                                        ALU.mult, ALU.add, sem=sv)
                p.wait("vector", sv, comb_done[k])
                if n == 15:
                    if ph == 1:
                        p.wait("vector", qts, tmp_store[0])
                    ev = p.op("vector", "tensor_copy", lst_b[2][:, :], accP[:, :],
                              sem=sv)
                    acc_evac[ph] = ev
                    p.wait("scalar", sv, ev)
                    if ph == 0:
                        dst = s_tmpa.rearrange("(p a) r -> p (a r)", p=128)
                    else:
                        dst = s_tmpb.rearrange("r (a c) -> (r a) c", a=4)
                    tmp_store[ph] = p.dma("scalar", qts, dst, lst_b[2][:, :])
                    if ph == 1:
                        psT_guard[0] = psT_guard[1] = ev

        if want("A3"):
            p.wait("sync", te, a2_done_t)
            p.wait("sync", qsw[0], p_st)
            wqe3o = W64.rearrange("p (kc n) -> p kc n", n=KV)
            p.dma("sync", qx2, wqe3o, s_wqe.rearrange("kc p n -> p kc n"))
            wqe_loaded = qx2.n
            qev = {}
            first_q = True
            last_mm = None
            for s in range(2):
                if s > 0:
                    p.wait("sync", te, last_mm)
                p.dma("sync", qxn[0], xnbuf3, s_xn[s].rearrange("kc p t -> p kc t"))
                ld = qxn[0].n
                p.wait("tensor", qxn[0], ld)
                p.wait("tensor", qx2, wqe_loaded)
                if first_q:
                    p.wait("tensor", sv, a2_done_v)
                    first_q = False
                for j in range(NH):
                    emit_comb_steps(2)
                    bank = j % 2
                    g = qev.get((s, j - 2)) if j >= 2 else qev.get((s - 1, j + 6))
                    if g is not None:
                        p.wait("tensor", ss, g)
                    mm = None
                    for kc in range(NKC):
                        mm = p.op("tensor", "matmul",
                                  psA[:, bank * 512:(bank + 1) * 512],
                                  W64[:, kc * KV + j * 128: kc * KV + (j + 1) * 128],
                                  xnbuf[:, kc * SCT:(kc + 1) * SCT],
                                  start=(kc == 0), stop=(kc == NKC - 1), sem=te)
                    last_mm = mm
                    p.wait("scalar", te, mm)
                    qev[(s, j)] = p.op("scalar", "activation",
                                       Q_fm[:, j * TOWN + s * SCT: j * TOWN + (s + 1) * SCT],
                                       psA[:, bank * 512:(bank + 1) * 512],
                                       AF.Copy, scale=INVSQ, sem=ss)
            emit_comb_steps(32)
            a3_done_t = te.n
            a3_done_s = ss.n
            if "Q" in tap:
                p.wait("sync", ss, a3_done_s)
                p.dma("sync", qt_, tap["Q"], Q_fm[:, :])

        if want("B"):
            p.wait("scalar", ql[0], lst_ld[30])
            p.wait("scalar", ql[1], lst_ld[31])
            p.wait("scalar", qts, tmp_store[1])
            p.wait("gpsimd", te, a3_done_t)
            wo3o = W64.rearrange("p (kc n) -> p kc n", n=D)
            wo3i = wo.rearrange("(kc p) n -> p kc n", p=128)
            for hv in range(2):
                p.dma("gpsimd", qw, wo3o[:, hv * 4:(hv + 1) * 4], wo3i[:, hv * 4:(hv + 1) * 4])
            wo_loaded = qw.n

            mx = stats[:, 8:9]
            nmx = stats[:, 9:10]
            sexp = stats[:, 10:11]
            rs = stats[:, 11:12]

            exp_done = {"last": a3_done_s}
            scale_done = [a3_done_s, a3_done_s]
            o_ev = {"last": None}
            wo_guard = [None, None]
            woev = {}
            xres_ld = {}
            ln2_done = {}

            def b_scores(qt, h):
                p.wait("tensor", ss, exp_done["last"])
                mm = None
                for n in range(4):
                    mm = p.op("tensor", "matmul",
                              psA[:, n * 512:(n + 1) * 512],
                              Q_fm[:, h * TOWN + qt * 128: h * TOWN + (qt + 1) * 128],
                              K_fm[:, h * TFULL + n * 512: h * TFULL + (n + 1) * 512],
                              start=True, stop=True, sem=te)
                return mm

            def b_softmax(qt, h, sc_done, pbuf):
                p.wait("vector", te, sc_done)
                v = p.op("vector", "tensor_reduce", nmx, psA[:, :], AX.X, ALU.max,
                         negate=True, sem=sv)
                p.wait("scalar", sv, v)
                p.wait("scalar", ss, scale_done[h % 2])
                e = p.op("scalar", "activation", pbuf[:, :], psA[:, :], AF.Exp,
                         bias=nmx, sem=ss, accum_out=sexp)
                exp_done["last"] = e
                p.wait("vector", ss, e)
                rv = p.op("vector", "reciprocal", rs, sexp, sem=sv)
                p.wait("scalar", sv, rv)
                p.wait("scalar", ss, e)
                r = p.op("scalar", "activation", pbuf[:, :], pbuf[:, :], AF.Identity,
                         scale=rs, sem=ss)
                scale_done[h % 2] = r
                return r

            def b_ptrans_and_o(h, ps_done, pbuf):
                mm = None
                for g in range(2):
                    bank = g % 2
                    p.wait("tensor", ss, ps_done)
                    if psT_guard[bank] is not None:
                        p.wait("tensor", sv, psT_guard[bank])
                    tv = None
                    for i in range(8):
                        blk = g * 8 + i
                        tv = p.op("tensor", "transpose",
                                  psT[:, bank * 1024 + i * 128: bank * 1024 + (i + 1) * 128],
                                  pbuf[:, blk * 128:(blk + 1) * 128], idn[:, :], sem=te)
                    p.wait("vector", te, tv)
                    dst = PT_sb.rearrange("p (b d) -> p b d", d=128)[:, g * 8:(g + 1) * 8]
                    psT_guard[bank] = p.op("vector", "tensor_copy", dst, psT4[:, bank], sem=sv)
                p.wait("tensor", sv, psT_guard[1])
                if o_ev["last"] is not None:
                    p.wait("tensor", ss, o_ev["last"])
                if wo_guard[1] is not None:
                    p.wait("tensor", sv, wo_guard[1])
                for blk in range(16):
                    mm = p.op("tensor", "matmul", psO[:, 0:128],
                              V_tm[:, blk * KV + h * 128: blk * KV + (h + 1) * 128],
                              PT_sb[:, blk * 128:(blk + 1) * 128],
                              start=(blk == 0), stop=(blk == 15), sem=te)
                p.wait("scalar", te, mm)
                o_ev["last"] = p.op("scalar", "activation", O_fm[:, h * 128:(h + 1) * 128],
                                    psO[:, 0:128], AF.Copy, sem=ss)

            xnew_st = {}
            ln2_trans = {}

            def emit_xres_load(qt):
                if qt >= 1:
                    p.wait("sync", sv, woev[qt - 1])
                    p.wait("sync", ss, ln2_done[qt - 1])
                    p.wait("sync", qt_, xnew_st[qt - 1])
                    p.wait("sync", te, ln2_trans[qt - 1])
                p.dma("sync", qs, xresA[:, :], x_own[qt * 128:(qt + 1) * 128, 0:2048])
                p.dma("sync", qs, xresB[:, :], x_own[qt * 128:(qt + 1) * 128, 2048:4096])
                xres_ld[qt] = qs.n

            def emit_wo(qt):
                p.wait("tensor", ss, o_ev["last"])
                p.wait("tensor", qw, wo_loaded)
                wops = [psW, psO]
                for cb in range(8):
                    bank = wops[cb % 2]
                    if wo_guard[cb % 2] is not None:
                        p.wait("tensor", sv, wo_guard[cb % 2])
                    mm = None
                    for kc in range(NH):
                        mm = p.op("tensor", "matmul", bank[:, 0:512],
                                  O_fm[:, kc * 128:(kc + 1) * 128],
                                  W64[:, kc * D + cb * 512: kc * D + (cb + 1) * 512],
                                  start=(kc == 0), stop=(kc == NH - 1), sem=te)
                    p.wait("vector", te, mm)
                    if cb == 0:
                        p.wait("vector", qs, xres_ld[qt])
                    xr = xresA if cb < 4 else xresB
                    co = (cb % 4) * 512
                    wo_guard[cb % 2] = p.op("vector", "tensor_add",
                                            xr[:, co:co + 512], xr[:, co:co + 512],
                                            bank[:, 0:512], sem=sv)
                woev[qt] = wo_guard[1]
                p.wait("sync", sv, woev[qt])
                p.dma("sync", qt_, s_xnew[qt * 128:(qt + 1) * 128, 0:2048], xresA[:, :])
                xnew_st[qt] = p.dma("sync", qt_, s_xnew[qt * 128:(qt + 1) * 128, 2048:4096],
                                    xresB[:, :])

            def emit_ln2_stats(qt):
                p.wait("scalar", sv, woev[qt])
                p.wait("vector", sv, woev[qt])
                p.wait("scalar", qt_, xnew_st[qt])
                if qt >= 1:
                    p.wait("scalar", te, ln2_trans[qt - 1])
                ln2_done[qt] = ln_chunk([(xresA[:, :], xn2dA[:, :]),
                                         (xresB[:, :], xn2dB[:, :])])

            def emit_ln2_trans(qt):
                s2, c2 = divmod(qt, 4)
                dumps = [xn2dA, xn2dB]
                for gg in range(4):
                    bank = gg % 2
                    p.wait("tensor", ss, ln2_done[qt])
                    if c2 == 0 and s2 >= 1 and gg == 0:
                        p.wait("vector", qxs, xn2_store[s2 - 1])
                    if psT_guard[bank] is not None:
                        p.wait("tensor", sv, psT_guard[bank])
                    tv = None
                    for i in range(8):
                        kc = gg * 8 + i
                        src = dumps[kc // 16]
                        tv = p.op("tensor", "transpose",
                                  psT[:, bank * 1024 + i * 128: bank * 1024 + (i + 1) * 128],
                                  src[:, (kc % 16) * 128:(kc % 16 + 1) * 128],
                                  idn[:, :], sem=te)
                    p.wait("vector", te, tv)
                    dst = xnbuf3[:, gg * 8:(gg + 1) * 8, c2 * 128:(c2 + 1) * 128]
                    psT_guard[bank] = p.op("vector", "tensor_copy", dst, psT4[:, bank], sem=sv)
                ln2_trans[qt] = te.n
                if c2 == 3:
                    p.wait("sync", sv)
                    xn2_store[s2] = p.dma("sync", qxs,
                                          s_xn2[s2].rearrange("kc p t -> p kc t"), xnbuf3)

            emit_xres_load(0)
            pend = None
            for qt in range(NQT):
                for h in range(NH):
                    sc = b_scores(qt, h)
                    if qt >= 1 and h == 1:
                        emit_wo(qt - 1)
                    if pend is not None:
                        b_ptrans_and_o(*pend)
                    pend_ps = b_softmax(qt, h, sc, P_tm[h % 2])
                    if qt == 0 and h == 0 and "P0" in tap:
                        p.wait("sync", ss, pend_ps)
                        p.dma("sync", qt_, tap["P0"], P_tm[0][:, :])
                        p.wait("scalar", qt_)
                    pend = (h, pend_ps, P_tm[h % 2])
                    if qt >= 1:
                        if h == 2:
                            emit_ln2_stats(qt - 1)
                        elif h == 5:
                            emit_ln2_trans(qt - 1)
                        elif h == 6:
                            emit_xres_load(qt)
            b_ptrans_and_o(*pend)
            o_last_t = te.n
            emit_wo(NQT - 1)
            emit_ln2_stats(NQT - 1)
            emit_ln2_trans(NQT - 1)
            b_done_t = te.n
            b_done_v = sv.n
            b_done_s = ss.n
            if "XNEW" in tap:
                p.wait("sync", qt_)
                p.dma("sync", qt_, tap["XNEW"], s_xnew)
            if "XN2" in tap:
                p.wait("sync", qt_)
                p.wait("sync", qxs)
                for s2 in range(2):
                    p.dma("sync", qt_, tap["XN2"][s2], s_xn2[s2])

        if want("C"):
            xnew_stores = qt_.n

            k3o = ker_bf.rearrange("p (kc n) -> p kc n", n=2048)
            q_loaded = {}
            ker_free = {}

            def emit_ker_quarter(hf, q):
                if hf == 0 and q == 0:
                    p.wait("gpsimd", te, o_last_t)
                if hf == 0 and q == 1:
                    p.wait("gpsimd", te, b_done_t)
                    p.wait("gpsimd", ss, b_done_s)
                    p.wait("gpsimd", sv, b_done_v)
                    p.wait("gpsimd", qt_, xnew_stores)
                if hf == 1:
                    p.wait("gpsimd", te, ker_free[q])
                k3i = ker.rearrange("(kc p) n -> p kc n", p=128)[:, :, hf * 2048:(hf + 1) * 2048]
                q_loaded[(hf, q)] = p.dma("gpsimd", qk[q], k3o[:, q * 8:(q + 1) * 8],
                                          k3i[:, q * 8:(q + 1) * 8])

            for q in range(4):
                emit_ker_quarter(0, q)
            p.wait("gpsimd", qts, tmp_store[1])
            acat3 = A_catT.rearrange("p (kc r) -> p kc r", r=R)
            p.dma("gpsimd", ql[0], acat3[:, :, 0:32], s_tmpa.rearrange("(kc p) r -> p kc r", p=128))
            p.dma("gpsimd", ql[0], acat3[:, :, 32:64], last.rearrange("(kc p) r -> p kc r", p=128))
            p.dma("gpsimd", ql[0], B_catT[0:32, :], s_tmpb)
            p.dma("gpsimd", ql[0], B_catT[32:64, :], lbst)
            lora_ready = ql[0].n

            woev2 = {}
            out_st = {}
            kermm_done = None
            ax_ready = {}
            loaded_s2 = [None]
            idx = 0
            for hf in range(2):
                if hf == 0:
                    p.wait("sync", te, b_done_t)
                    p.wait("sync", ss, b_done_s)
                    p.wait("sync", sv, b_done_v)
                    p.wait("sync", qt_, xnew_stores)
                p.dma("sync", qsw[hf], bias_h[:, :], bias_b[:, hf * 2048:(hf + 1) * 2048])
                bias_ld = qsw[hf].n
                for s2 in ((0, 1) if hf == 0 else (1, 0)):
                    if loaded_s2[0] != s2:
                        if (hf, s2) != (0, 0):
                            p.wait("sync", te, kermm_done)
                        if hf == 0 and want("B"):
                            p.wait("sync", qxs, max(xn2_store.values()))
                        p.dma("sync", qx2, xnbuf3, s_xn2[s2].rearrange("kc p t -> p kc t"))
                        xn2_ld = qx2.n
                        loaded_s2[0] = s2
                    for t in range(4):
                        chunk = s2 * 4 + t
                        g = woev2.get(idx - 2)
                        if g is not None:
                            p.wait("sync", sv, g)
                            p.wait("sync", qo, 16 * idx)
                        p.dma("sync", qb[idx % 2], xnb[idx % 2][:, :],
                              s_xnew[chunk * 128:(chunk + 1) * 128, hf * 2048:(hf + 1) * 2048])
                        xnew_ld = qb[idx % 2].n
                        p.wait("tensor", qx2, xn2_ld)
                        g = woev2.get(idx - 1)
                        if g is not None:
                            p.wait("tensor", sv, g)
                        if idx == 0:
                            p.wait("tensor", ss, b_done_s)
                        mm = None
                        for kc in range(NKC):
                            if kc % 8 == 0:
                                p.wait("tensor", qk[kc // 8], q_loaded[(hf, kc // 8)])
                            lhs = xnbuf[:, kc * SCT + t * 128: kc * SCT + (t + 1) * 128]
                            for cb in range(4):
                                mm = p.op("tensor", "matmul", psA[:, cb * 512:(cb + 1) * 512],
                                          lhs,
                                          ker_bf[:, kc * 2048 + cb * 512: kc * 2048 + (cb + 1) * 512],
                                          start=(kc == 0), stop=False, sem=te)
                            if hf == 0 and idx == 7 and kc % 8 == 7:
                                ker_free[kc // 8] = te.n
                                emit_ker_quarter(1, kc // 8)
                        if hf == 0 and t == 0:
                            p.wait("tensor", ql[0], lora_ready)
                            if s2 == 0:
                                p.wait("tensor", sv, b_done_v)
                            else:
                                p.wait("tensor", ss, ax_ready[1 - s2])
                            axm = None
                            for kc in range(NKC):
                                axm = p.op("tensor", "matmul", psW[0:64, 0:512],
                                           A_catT[:, kc * R:(kc + 1) * R],
                                           xnbuf[:, kc * SCT:(kc + 1) * SCT],
                                           start=(kc == 0), stop=(kc == NKC - 1), sem=te)
                            p.wait("scalar", te, axm)
                            if s2 == 0:
                                p.wait("scalar", qt_, xnew_stores)
                            ax_ready[s2] = p.op("scalar", "activation",
                                                Ax_bf[:, s2 * SCT:(s2 + 1) * SCT],
                                                psW[0:64, 0:512], AF.Copy, sem=ss)
                        p.wait("tensor", ss, ax_ready[s2])
                        for cb in range(4):
                            mm = p.op("tensor", "matmul", psA[:, cb * 512:(cb + 1) * 512],
                                      Ax_bf[:, chunk * 128:(chunk + 1) * 128],
                                      B_catT[:, hf * 2048 + cb * 512: hf * 2048 + (cb + 1) * 512],
                                      start=False, stop=True, sem=te)
                        kermm_done = mm
                        p.wait("vector", te, kermm_done)
                        p.wait("vector", qb[idx % 2], xnew_ld)
                        p.wait("vector", qsw[hf], bias_ld)
                        av = None
                        for cb in range(4):
                            av = p.op("vector", "tensor_add",
                                      xnb[idx % 2][:, cb * 512:(cb + 1) * 512],
                                      xnb[idx % 2][:, cb * 512:(cb + 1) * 512],
                                      psA[:, cb * 512:(cb + 1) * 512], sem=sv)
                        p.wait("vector", sv, av)
                        woev2[idx] = p.op("vector", "tensor_add", xnb[idx % 2][:, :],
                                          xnb[idx % 2][:, :], bias_h[:, :], sem=sv)
                        p.wait("sync", sv, woev2[idx])
                        out_st[idx] = p.dma("sync", qo,
                                            out[chunk * 128:(chunk + 1) * 128,
                                                hf * 2048:(hf + 1) * 2048],
                                            xnb[idx % 2][:, :])
                        idx += 1

        p.wait("sync", qt_)
        p.wait("sync", qo)
        p.wait("sync", qxs)

        with nc.Block() as block:
            @block.sync
            def _(e):
                for f in p.q["sync"]:
                    f(e)

            @block.scalar
            def _(e):
                for f in p.q["scalar"]:
                    f(e)

            @block.vector
            def _(e):
                for f in p.q["vector"]:
                    f(e)

            @block.tensor
            def _(e):
                for f in p.q["tensor"]:
                    f(e)

            @block.gpsimd
            def _(e):
                for f in p.q["gpsimd"]:
                    f(e)

    return nc


def make_in_maps(inputs):
    X = np.asarray(inputs["L0_attn_x"], dtype=np.float32)
    wq = np.ascontiguousarray(np.asarray(inputs["L0_attn_wq"], np.float32))
    wk = np.ascontiguousarray(np.asarray(inputs["L0_attn_wk"], np.float32))
    wv = np.ascontiguousarray(np.asarray(inputs["L0_attn_wv"], np.float32))
    wo = np.ascontiguousarray(np.asarray(inputs["L0_attn_wo"], np.float32))
    gate = np.asarray(inputs["L0_ffn_gate"], np.float32)
    ker = np.ascontiguousarray(np.asarray(inputs["L0_ffn_kernel"], np.float32))
    bias = np.asarray(inputs["L0_ffn_bias"], np.float32)
    lA = np.asarray(inputs["L0_ffn_lora_A"], np.float32)
    lB = np.asarray(inputs["L0_ffn_lora_B"], np.float32)
    lAs = np.asarray(inputs["L0_ffn_lora_A_shared"], np.float32)
    lBs = np.asarray(inputs["L0_ffn_lora_B_shared"], np.float32)

    lat = np.ascontiguousarray(lA.transpose(0, 2, 1)).reshape(16, 128, 1024)
    lbt = np.ascontiguousarray(lB.transpose(0, 2, 1)).reshape(16, 128, 1024)
    last = np.ascontiguousarray(lAs.T)
    lbst = np.ascontiguousarray(lBs.T)
    bias_b = np.ascontiguousarray(np.broadcast_to(bias, (128, 4096)))

    in_maps = []
    for c in range(8):
        b, h = divmod(c, 2)
        gate_b = np.ascontiguousarray(np.broadcast_to(gate[2 * b + h], (128, 16)))
        in_maps.append({
            "x_own": np.ascontiguousarray(X[b, h * 1024:(h + 1) * 1024]),
            "x_oth": np.ascontiguousarray(X[b, (1 - h) * 1024:(2 - h) * 1024]),
            "wq": wq, "wk": wk, "wv": wv, "wo": wo, "ker": ker,
            "bias_b": bias_b, "gate_b": gate_b,
            "lat": lat, "lbt": lbt, "last": last, "lbst": lbst,
        })
    return in_maps


def kernel(**inputs):
    nc = build_nc()
    in_maps = make_in_maps(inputs)
    res = run_bass_kernel_spmd(nc, in_maps, core_ids=list(range(8)))
    out = np.empty((4, 2048, 4096), np.float32)
    for c, r in enumerate(res.results):
        b, h = divmod(c, 2)
        out[b, h * 1024:(h + 1) * 1024] = r["out"]
    return out
